# version 28
# baseline (speedup 1.0000x reference)
import numpy as np
import concourse.bass as bass
import concourse.mybir as mybir
from concourse.alu_op_type import AluOpType as A
from concourse.bass_utils import run_bass_kernel_spmd

F32 = mybir.dt.float32
BF16 = mybir.dt.bfloat16
I32 = mybir.dt.int32
AF = mybir.ActivationFunctionType

D, SEQ, H, HD, DH = 1024, 2048, 16, 64, 4096
ST = 256
NST = SEQ // ST
FT = 256
NFT = SEQ // FT
CH = 128
EPS_DEN = 1e-6
LN_EPS = 1e-5

ENG = ("sync", "gpsimd", "scalar", "vector", "tensor")


class Sch:
    def __init__(self, nc):
        self.nc = nc
        self.st = {e: [] for e in ENG}
        self.c = {}
        self.sems = {}
        self.bankdep = {}

    def S(self, name):
        if name not in self.sems:
            self.sems[name] = self.nc.alloc_semaphore(name)
            self.c[name] = 0
        return self.sems[name]

    def do(self, eng, fn):
        self.st[eng].append(fn)

    def w(self, eng, sem, val=None):
        sm = self.S(sem)
        v = self.c.get(sem, 0) if val is None else val
        if v > 0:
            self.do(eng, lambda e, sm=sm, v=v: e.wait_ge(sm, v))
        return v

    def op(self, eng, build, inc=None, dr=False):
        if dr:
            self.do(eng, lambda e: e.drain())
        if inc is not None:
            sm = self.S(inc)
            self.c[inc] += 1
            self.do(eng, lambda e, b=build, sm=sm: b(e).then_inc(sm, 1))
            return self.c[inc]
        self.do(eng, build)
        return None

    def dma(self, eng, sem, out, in_, slow=False, transpose=False):
        sm = self.S(sem)
        self.c[sem] += 16
        self.do(eng, lambda e, o=out, i=in_, sm=sm, sl=slow, tp=transpose:
                e.dma_start(out=o, in_=i, transpose=tp, allow_slow_non_contiguous=sl).then_inc(sm, 16))
        return self.c[sem]

    def bank_wait(self, eng, *banks):
        merged = {}
        for b in banks:
            for sem, val in self.bankdep.get(b, ()):
                merged[sem] = max(merged.get(sem, 0), val)
        for sem, val in merged.items():
            self.w(eng, sem, val)

    def bank_set(self, banks, *marks):
        for b in banks:
            self.bankdep[b] = [m for m in marks if m[1]]

    def barrier(self, tag):
        marks = {}
        sems = {"sync": "BSY", "gpsimd": "BG", "scalar": "BSC", "vector": "BV", "tensor": "BPE"}
        for eng, sm in sems.items():
            marks[eng] = self.op(eng, lambda e: e.nop(), inc=sm, dr=True)
        for eng in ENG:
            for oeng, sm in sems.items():
                if oeng != eng:
                    self.w(eng, sm, marks[oeng])


class SBAlloc:
    BASE = 24576
    TOP = 229376

    def __init__(self, nc):
        self.nc = nc
        self.off = self.BASE
        self.marks = {}

    def mark(self, name):
        self.marks[name] = self.off

    def reset_to(self, name):
        self.off = self.marks[name]

    def al(self, name, shape, dt):
        per_part = int(np.prod(shape[1:])) * mybir.dt.size(dt)
        per_part = (per_part + 31) // 32 * 32
        assert self.off + per_part <= self.TOP, f"SBUF overflow at {name}: {self.off}+{per_part}"
        h = self.nc.alloc_sbuf_tensor_at(name, list(shape), dt, offset=self.off)
        self.off += per_part
        return h


def build_program(stage=4, taps=(), n_stiles=NST, n_ftiles=NFT, sub=99):
    nc = bass.Bass("TRN2", target_bir_lowering=False)
    P = Sch(nc)
    for _sm in ("dS", "dG", "dGw", "dC", "dX", "dW0", "dW1", "dW2", "dW3", "dW4", "dW5", "dQ0", "dQ1", "dQ2", "dQ3", "dH", "dWq", "dWq2", "dSp", "dOut", "dPf0", "dPf1", "PE", "V", "SC", "G"):
        P.S(_sm)
    sb = SBAlloc(nc)

    dx = nc.dram_tensor("x", [SEQ, D], F32, kind="ExternalInput").ap()
    dxT = nc.dram_tensor("xT", [D, SEQ], F32, kind="ExternalInput").ap()
    dW = {nm: nc.dram_tensor(nm, [D, D], F32, kind="ExternalInput").ap()
          for nm in ("WqT", "WkT", "WvT", "WoT")}
    db = {nm: nc.dram_tensor(nm, [D], F32, kind="ExternalInput").ap()
          for nm in ("bq", "bk", "bv", "bo")}
    dW1T = nc.dram_tensor("W1T", [D, DH], F32, kind="ExternalInput").ap()
    dW2T = nc.dram_tensor("W2T", [DH, D], F32, kind="ExternalInput").ap()
    dconvw = nc.dram_tensor("convw", [128, 96], F32, kind="ExternalInput").ap()
    dconvb = nc.dram_tensor("convb", [128, 32], F32, kind="ExternalInput").ap()
    dout = nc.dram_tensor("out", [SEQ, D], F32, kind="ExternalOutput").ap()
    d_r1sp = nc.dram_tensor("r1_spill", [SEQ, D], F32, kind="Internal").ap()
    d_xq2 = nc.dram_tensor("xq_spill_tm", [SEQ + 16, D], BF16, kind="Internal").ap()
    d_colsp = nc.dram_tensor("colsc_spill", [1, SEQ + 2], F32, kind="Internal").ap()
    d_w1qsp = nc.dram_tensor("w1q_spill", [D, DH], BF16, kind="Internal").ap()
    d_w2qsp = nc.dram_tensor("w2q_spill", [DH, D], BF16, kind="Internal").ap()
    dtap = {}
    for t in taps:
        shp = {
            "qT": ([D, SEQ], BF16), "kT": ([D, SEQ], BF16),
            "kt": ([SEQ, D], BF16), "v": ([SEQ, D], BF16),
            "attn": ([SEQ, D], BF16), "r1ln": ([SEQ, D], F32),
            "xq": ([SEQ, D], BF16), "wq": ([D, DH], BF16),
            "hsil": ([128, 32 * (FT + 2)], BF16),
            "hq": ([128, 32 * (FT + 2)], BF16),
        }[t]
        dtap[t] = nc.dram_tensor("tap_" + t, list(shp[0]), shp[1], kind="ExternalOutput").ap()

    ident = sb.al("ident", [128, 128], BF16).ap()
    identf = sb.al("identf", [128, 128], F32).ap()
    ones_col = sb.al("ones_col", [128, 1], BF16).ap()
    ones_rowf = sb.al("ones_rowf", [1, 128], F32).ap()
    ones_colf = sb.al("ones_colf", [128, 1], F32).ap()
    zeros_bf = sb.al("zeros_bf", [128, 2], BF16).ap()
    zeros_f = sb.al("zeros_f", [128, 2], F32).ap()
    convw_sb = sb.al("convw_sb", [128, 96], F32).ap()
    convb_sb = sb.al("convb_sb", [128, 32], F32).ap()
    stn = ("m", "u", "rstd", "mx2", "sa", "colraw", "t0", "t1", "denf", "denr",
           "mhA", "s2", "ev2", "m2", "u2", "rstd2", "t2", "t3", "nbm", "eps")
    st = {k: sb.al("st_" + k, [128, 16], F32).ap() for k in stn}
    wabs_acc = sb.al("wabs_acc", [128, 64], F32).ap()
    wsc = sb.al("wsc", [1, 12], F32).ap()
    sc_stat1 = sb.al("sc_stat1", [1, 128], F32).ap()
    colrow_sb = sb.al("colrow_sb", [16, 128], F32).ap()
    nb2_pp = sb.al("nb2_pp", [128, 1], F32).ap()
    thr_pp = sb.al("thr_pp", [128, 4], F32).ap()
    sb.mark("phase")

    wT = {nm: sb.al(nm + "_bf", [128, 8 * D], BF16).ap() for nm in ("WqT", "WkT", "WvT", "WoT")}
    mask4 = sb.al("mask4", [128, 512], F32).ap()
    bq_pp = sb.al("bq_pp", [128, 8], F32).ap()
    bk_pp = sb.al("bk_pp", [128, 8], F32).ap()
    brow_t = sb.al("brow_t", [65, D], F32)
    brow = {nm: brow_t.ap()[32 * i:32 * i + 1, :] for i, nm in enumerate(("bk", "bv", "bo"))}
    ones3 = sb.al("ones3", [65, 128], F32).ap()
    bk_bc = sb.al("bk_bc", [128, D], F32).ap()
    bv_bc = sb.al("bv_bc", [128, D], F32).ap()
    bo_bc = sb.al("bo_bc", [128, D], F32).ap()
    x_t = sb.al("x_t", [128, 2 * D], F32).ap()
    r1ln_sb = x_t
    xT_bf = sb.al("xT_bf", [128, 8 * ST], BF16).ap()
    qT_sb = sb.al("qT_sb", [128, 8 * ST], BF16).ap()
    S_e = sb.al("S_e", [128, 8 * 65], BF16).ap()
    S_o = sb.al("S_o", [128, 8 * 65], BF16).ap()
    kT_e = sb.al("kT_e", [128, 8 * ST], BF16).ap()
    kT_o = sb.al("kT_o", [128, 8 * ST], BF16).ap()
    kt_sb = sb.al("kt_sb", [128, 2 * D], BF16).ap()
    v_sb = sb.al("v_sb", [128, 2 * 16 * 65], BF16).ap()
    At_sb = sb.al("At_sb", [128, 16 * CH], BF16).ap()
    attn_sb = sb.al("attn_sb", [128, 2 * D], BF16).ap()
    attnT_sb = sb.al("attnT_sb", [128, 8 * ST], BF16).ap()
    xqT_bf = attnT_sb
    r1_sb = sb.al("r1_sb", [128, 2 * D], F32).ap()
    xq_i32 = sb.al("xq_i32", [128, D], I32).ap()
    xq_bf = sb.al("xq_bf", [128, 2 * D], BF16).ap()
    exp_sc = sb.al("exp_sc", [128, D], F32).ap()
    relu_sc = sb.al("relu_sc", [128, D], F32).ap()
    tmp1 = sb.al("tmp1", [128, D], F32).ap()
    wfp2 = [sb.al(f"wfp{i}", [128, 1024], F32).ap() for i in range(5)]
    wq_a = sb.al("wq_a", [128, 1024], BF16).ap()
    wab_o = wq_a
    wq_o = [sb.al(f"wq_o{i}", [128, 1024], BF16).ap() for i in range(4)]
    A_END = sb.off

    sb.reset_to("phase")
    WF = FT + 2
    W1q_sb = sb.al("W1q_sb", [128, 8 * DH], BF16).ap()
    W2q_sb = sb.al("W2q_sb", [128, 32 * D], BF16).ap()
    h_sb_h = sb.al("h_sb", [128, 32 * WF], BF16)
    h_sb = h_sb_h.ap()
    r2sq_scr = h_sb_h.bitcast(F32).ap()[:, 0:D]
    XW = FT + 16
    xqTb2 = [sb.al(f"xqTb{i}", [128, 8 * XW], BF16).ap() for i in range(2)]
    r1b2 = [sb.al(f"r1b_{i}", [128, 2 * D], F32).ap() for i in range(2)]
    scrU_h = sb.al("scrU", [128, D], F32)
    r2_sb = scrU_h.ap()
    q_scr = scrU_h.ap()[:, 0:2 * FT]
    hq_i32 = scrU_h.bitcast(I32).ap()[:, 2 * FT:4 * FT]
    colsc_row2 = [sb.al(f"colsc_row{i}", [1, WF], F32).ap() for i in range(2)]
    colsc_bc = sb.al("colsc_bc", [128, WF], F32).ap()
    s2_row = sb.al("s2_row", [1, FT], F32).ap()
    s2_bc = sb.al("s2_bc", [128, FT], F32).ap()
    maccP = [sb.al(f"maccP{i}", [128, FT], BF16).ap() for i in range(2)]
    maccN = [sb.al(f"maccN{i}", [128, FT], BF16).ap() for i in range(2)]
    diag0_all = sb.al("diag0_all", [128, 32 * 128], BF16).ap()
    ln2_junk = sb.al("ln2_junk", [128, D], F32).ap()
    B_END = sb.off

    ppair = [nc.alloc_psum_tensor(f"pp{i}", [128, 1024], F32) for i in range(4)]
    pairA = [p.ap() for p in ppair]
    pairB = [p.bitcast(BF16).ap() for p in ppair]
    pba = [pairA[i // 2][:, 512 * (i % 2):512 * (i % 2 + 1)] for i in range(8)]
    pbfa = [pairB[i // 2][:, 1024 * (i % 2):1024 * (i % 2 + 1)] for i in range(8)]

    Tm = lambda b, inc=None: P.op("tensor", b, inc)
    Vm = lambda b, inc=None: P.op("vector", b, inc, dr=True)
    Sm = lambda b, inc=None: P.op("scalar", b, inc, dr=True)
    Gm = lambda b, inc=None: P.op("gpsimd", b, inc, dr=True)

    def prologue():
        import os as _os
        if _os.environ.get("DK_NOCONST"):
            Gm(lambda e: e.memset(ident, 0.0), inc="G")
        else:
            _prologue_consts()
        if not _os.environ.get("DK_NOWL"):
            _prologue_wl()
        return

    def _prologue_consts():
        import os as _os
        Gm(lambda e: e.memset(ident, 0.0))
        Gm(lambda e: e.affine_select(out=ident, in_=ident, compare_op=A.not_equal,
                                     fill=1.0, base=0, pattern=[[-1, 128]], channel_multiplier=1))
        Gm(lambda e: e.memset(identf, 0.0))
        Gm(lambda e: e.affine_select(out=identf, in_=identf, compare_op=A.not_equal,
                                     fill=1.0, base=0, pattern=[[-1, 128]], channel_multiplier=1))
        Gm(lambda e: e.memset(mask4, 1.0))
        for j in range(4):
            sl = mask4[:, 128 * j:128 * (j + 1)]
            Gm(lambda e, sl=sl: e.affine_select(out=sl, in_=sl, compare_op=A.is_ge,
                                                fill=0.0, base=0, pattern=[[1, 128]], channel_multiplier=-1))
        Gm(lambda e: e.memset(ones_col, 1.0))
        Gm(lambda e: e.memset(ones_rowf, 1.0))
        Gm(lambda e: e.memset(ones_colf, 1.0))
        Gm(lambda e: e.memset(st["eps"][:, 0:1], LN_EPS))
        Gm(lambda e: e.memset(ones3, 1.0))
        Gm(lambda e: e.memset(zeros_bf, 0.0))
        Gm(lambda e: e.memset(S_e, 0.0))
        Gm(lambda e: e.memset(S_o, 0.0))
        Gm(lambda e: e.memset(kT_e, 0.0))
        Gm(lambda e: e.memset(kT_o, 0.0))
        Gm(lambda e: e.memset(v_sb.rearrange("p (l h c) -> p l h c", l=2, c=65)[:, :, :, 64:65], 1.0))
        Gm(lambda e: e.memset(zeros_f, 0.0), inc="G")

    def _prologue_wl():
        import os as _os
        gp_const = P.c["G"]
        for nm in ("WqT", "WkT", "WvT", "WoT"):
            for kc in range(8):
                P.dma("gpsimd", "dGw", wT[nm][:, D * kc:D * (kc + 1)], dW[nm][128 * kc:128 * (kc + 1), :])
        for nm, dst in (("bq", bq_pp), ("bk", bk_pp)):
            for oc in range(8):
                P.dma("gpsimd", "dGw", dst[:, oc:oc + 1], db[nm][128 * oc:128 * (oc + 1)])
        for nm in ("bk", "bv", "bo"):
            P.dma("gpsimd", "dGw", brow[nm], db[nm][:])
        P.dma("gpsimd", "dGw", convw_sb, dconvw)
        P.dma("gpsimd", "dGw", convb_sb, dconvb)
        dg_pro = P.c["dGw"]

        P.w("scalar", "G", gp_const)
        for kc in range(0 if _os.environ.get("DK_NOEDGE") else 8):
            pass
        P.dma("scalar", "dC", d_colsp[0:1, 0:1], zeros_f[0:1, 0:1])
        P.dma("scalar", "dC", d_colsp[0:1, SEQ + 1:SEQ + 2], zeros_f[0:1, 0:1])
        Gm(lambda e: e.memset(xq_bf[:, 0:D], 0.0), inc="G")
        P.w("scalar", "G", P.c["G"])
        P.dma("scalar", "dC", d_xq2[0:1, :], xq_bf[0:1, 0:D])
        P.dma("scalar", "dC", d_xq2[SEQ + 1:SEQ + 16, :], xq_bf[0:15, 0:D])

        P.w("tensor", "dGw", dg_pro)
        P.w("tensor", "G", gp_const)
        if _os.environ.get("DK_NOBC"):
            return
        for i, nm in enumerate(("bk", "bv", "bo")):
            for hf in range(2):
                Tm(lambda e, b=pba[2 * i + hf], s=brow[nm], one=ones3[32 * i:32 * i + 1, :], hf=hf:
                   e.matmul(b[:, 0:512], one, s[:, 512 * hf:512 * (hf + 1)], start=True, stop=True),
                   inc="PE" if (i == 2 and hf == 1) else None)
        P.w("vector", "PE", P.c["PE"])
        for i, dst in enumerate((bk_bc, bv_bc, bo_bc)):
            for hf in range(2):
                Vm(lambda e, d=dst, b=pba[2 * i + hf], hf=hf:
                   e.tensor_copy(out=d[:, 512 * hf:512 * (hf + 1)], in_=b[:, 0:512]),
                   inc="V" if (i == 2 and hf == 1) else None)
        P.w("tensor", "V", P.c["V"])

    WJOBS = [("W1", kc, hf) for kc in range(8) for hf in range(4)] + [("W2", kc, 0) for kc in range(32)]
    NWR = 5
    WQ_AHEAD = 4
    wq_ring = [("SC", 0)] * NWR
    wq_dv = {}
    wq_ld = [0]
    wq_osp = [0, 0, 0, 0]

    def wq_src(idx):
        which, kc, hf = WJOBS[idx]
        if which == "W1":
            return dW1T[128 * kc:128 * (kc + 1), 1024 * hf:1024 * (hf + 1)]
        return dW2T[128 * kc:128 * (kc + 1), :]

    def wq_issue_one():
        L = wq_ld[0]
        slot = L % NWR
        eng, mk = wq_ring[slot]
        if mk:
            P.w("sync", eng, mk)
        wq_dv[L] = P.dma("sync", "dW%d" % slot, wfp2[slot], wq_src(L if L < 64 else L - 64))
        wq_ld[0] = L + 1

    def wq_pump(comps):
        for c in comps:
            ld_needed = c if c < 64 else c - 1
            while wq_ld[0] < min(ld_needed + 1 + WQ_AHEAD, 128):
                wq_issue_one()
            if c < 64:
                wq_pass1_job(c)
            elif c == 64:
                wq_finalize()
            else:
                wq_pass2_job(c - 65)

    def wq_pass1_job(idx):
        slot = idx % NWR
        buf = wfp2[slot]
        P.w("scalar", "dW%d" % slot, wq_dv[idx])
        m = Sm(lambda e, col=idx, buf=buf: e.activation(out=wab_o[:, 0:1024], in_=buf,
                                                        func=AF.Abs, accum_out=wabs_acc[:, col:col + 1]), inc="SC")
        wq_ring[slot] = ("SC", m)

    def wq_finalize():
        P.w("vector", "SC", P.c["SC"])
        Vm(lambda e: e.reduce_sum(out=st["t0"][:, 0:1], in_=wabs_acc[:, 0:32], axis=mybir.AxisListType.X))
        fv = Vm(lambda e: e.reduce_sum(out=st["t0"][:, 1:2], in_=wabs_acc[:, 32:64], axis=mybir.AxisListType.X), inc="V")
        P.bank_wait("tensor", 4)
        P.w("tensor", "V", fv)
        fm = Tm(lambda e: e.matmul(pba[4][0:1, 0:2], ones_colf, st["t0"][:, 0:2], start=True, stop=True), inc="PE")
        P.w("vector", "PE", fm)
        Vm(lambda e: e.tensor_scalar(out=wsc[0:1, 0:2], in0=pba[4][0:1, 0:2], scalar1=1.0 / (D * DH), scalar2=None, op0=A.mult))
        Vm(lambda e: e.tensor_scalar(out=wsc[0:1, 2:4], in0=wsc[0:1, 0:2], scalar1=1e-5, scalar2=None, op0=A.max))
        Vm(lambda e: e.tensor_scalar(out=wsc[0:1, 4:5], in0=wsc[0:1, 2:3], scalar1=0.5, scalar2=None, op0=A.mult))
        Vm(lambda e: e.tensor_scalar(out=wsc[0:1, 5:6], in0=wsc[0:1, 2:3], scalar1=-0.5, scalar2=None, op0=A.mult))
        Vm(lambda e: e.tensor_scalar(out=wsc[0:1, 6:7], in0=wsc[0:1, 3:4], scalar1=0.5, scalar2=None, op0=A.mult))
        Vm(lambda e: e.tensor_scalar(out=wsc[0:1, 7:8], in0=wsc[0:1, 3:4], scalar1=-0.5, scalar2=None, op0=A.mult))
        Vm(lambda e: e.tensor_scalar(out=wsc[0:1, 8:10], in0=wsc[0:1, 2:4], scalar1=-1.0 / 127.0, scalar2=None, op0=A.mult))
        Vm(lambda e: e.tensor_scalar(out=wsc[0:1, 10:11], in0=wsc[0:1, 2:3], scalar1=-1.0, scalar2=None, op0=A.mult))
        fv = Vm(lambda e: e.tensor_scalar(out=sc_stat1, in0=ones_rowf, scalar1=wsc[0:1, 10:11], scalar2=None, op0=A.mult), inc="V")
        P.w("tensor", "V", fv)
        fm = Tm(lambda e: e.matmul(pba[4][:, 0:6], ones_rowf, wsc[0:1, 4:10], start=True, stop=True), inc="PE")
        P.w("vector", "PE", fm)
        Vm(lambda e: e.tensor_copy(out=thr_pp, in_=pba[4][:, 0:4]))
        fv = Vm(lambda e: e.tensor_copy(out=nb2_pp, in_=pba[4][:, 5:6]), inc="V")
        P.bank_set([4], ("V", fv))

    def wq_pass2_job(idx):
        which, kc, hf = WJOBS[idx]
        dst = (d_w1qsp[128 * kc:128 * (kc + 1), 1024 * hf:1024 * (hf + 1)] if which == "W1"
               else d_w2qsp[128 * kc:128 * (kc + 1), :])
        thr = thr_pp[:, 0:1] if which == "W1" else thr_pp[:, 2:3]
        nthr = thr_pp[:, 1:2] if which == "W1" else thr_pp[:, 3:4]
        slot = (64 + idx) % NWR
        buf = wfp2[slot]
        oslot = idx % 4
        obuf = wq_o[oslot]
        if idx % 2 == 1:
            P.w("scalar", "dW%d" % slot, wq_dv[64 + idx])
            P.w("scalar", "dQ%d" % oslot, wq_osp[oslot])
            Sm(lambda e, buf=buf, ob=obuf: e.activation(out=ob[:, 0:1024], in_=buf, func=AF.Sign, scale=-1.0))
            s2 = Sm(lambda e, buf=buf: e.activation(out=buf, in_=buf, func=AF.Abs), inc="SC")
            P.w("vector", "SC", s2)
            Vm(lambda e, thr=thr, buf=buf: e.tensor_scalar(out=wq_a[:, 0:1024], in0=buf, scalar1=thr, scalar2=None, op0=A.is_ge))
            gm = Vm(lambda e, ob=obuf: e.tensor_tensor(out=ob[:, 0:1024], in0=ob[:, 0:1024], in1=wq_a[:, 0:1024], op=A.mult), inc="V")
        else:
            P.w("vector", "dW%d" % slot, wq_dv[64 + idx])
            P.w("vector", "dQ%d" % oslot, wq_osp[oslot])
            Vm(lambda e, thr=thr, buf=buf: e.tensor_scalar(out=wq_a[:, 0:1024], in0=buf, scalar1=thr, scalar2=None, op0=A.is_ge))
            gm = Vm(lambda e, nthr=nthr, buf=buf, ob=obuf: e.scalar_tensor_tensor(out=ob[:, 0:1024], in0=buf, scalar=nthr,
                                                                                  in1=wq_a[:, 0:1024], op0=A.is_le, op1=A.subtract), inc="V")
        wq_ring[slot] = ("V", gm)
        P.w("sync", "V", gm)
        wq_osp[oslot] = P.dma("sync", "dQ%d" % oslot, dst, obuf[:, 0:1024])
        if "wq" in dtap and which == "W1":
            wq_osp[oslot] = P.dma("sync", "dQ%d" % oslot, dtap["wq"][128 * kc:128 * (kc + 1), 1024 * hf:1024 * (hf + 1)], obuf[:, 0:1024])


    def opsum(h):
        b, sl = (4 + h // 7, h % 7) if h < 14 else (3, h - 14)
        return pba[b][:, 65 * sl:65 * sl + 65]

    v3 = v_sb.rearrange("p (l h c) -> p l h c", l=2, c=65)

    last = {}
    tail_pend = []

    def tail_pump(n):
        for _ in range(n):
            if tail_pend:
                tail_pend.pop(0)()

    def tail_flush():
        while tail_pend:
            tail_pend.pop(0)()

    ln2_pend = []

    def ln2_pump(n):
        for _ in range(n):
            if ln2_pend:
                ln2_pend.pop(0)()

    def ln2_flush():
        while ln2_pend:
            ln2_pend.pop(0)()

    def stile(i, wqjobs=()):
        wqq = list(wqjobs)
        t0 = ST * i

        P.w("gpsimd", "PE", last.get("vproj", 0))
        for kc in range(8):
            P.dma("gpsimd", "dG", xT_bf[:, ST * kc:ST * (kc + 1)], dxT[128 * kc:128 * (kc + 1), t0:t0 + ST])
        xg = P.c["dG"]

        P.w("tensor", "dGw")
        P.w("tensor", "dG", xg)
        slot_mark = [0, 0]
        for nm, bpp in (("WqT", bq_pp), ("WkT", bk_pp)):
            for oc in range(8):
                b = (6, 7, 2, 3, 4, 5)[oc % 6]
                P.bank_wait("tensor", b)
                pe_v = None
                for kc in range(8):
                    stat = wT[nm][:, D * kc + 128 * oc:D * kc + 128 * (oc + 1)]
                    mov = xT_bf[:, ST * kc:ST * (kc + 1)]
                    pe_v = Tm(lambda e, o=pba[b], s=stat, m=mov, st_=(kc == 0), sp=(kc == 7):
                              e.matmul(o[:, 0:ST], s, m, start=st_, stop=sp),
                              inc="PE" if kc == 7 else None) or pe_v
                scr = exp_sc[:, ST * (oc % 2):ST * (oc % 2 + 1)]
                rcr = relu_sc[:, ST * (oc % 2):ST * (oc % 2 + 1)]
                bia = bpp[:, oc:oc + 1]
                P.w("scalar", "PE", pe_v)
                P.w("scalar", "V", slot_mark[oc % 2])
                Sm(lambda e, o=scr, p=pba[b], b_=bia: e.activation(out=o, in_=p[:, 0:ST], func=AF.Exp, bias=b_))
                sc_m = Sm(lambda e, o=rcr, p=pba[b], b_=bia: e.activation(out=o, in_=p[:, 0:ST], func=AF.Relu, bias=b_), inc="SC")
                P.w("vector", "SC", sc_m)
                cs = slice(ST * oc, ST * (oc + 1))
                if nm == "WqT":
                    vm = Vm(lambda e, o=qT_sb[:, cs], sc=scr, rc=rcr:
                            e.scalar_tensor_tensor(out=o, in0=sc, scalar=1.0, in1=rc, op0=A.min, op1=A.add), inc="V")
                else:
                    Vm(lambda e, o=kT_e[0:64, cs], sc=scr[0:64, :], rc=rcr[0:64, :]:
                       e.scalar_tensor_tensor(out=o, in0=sc, scalar=1.0, in1=rc, op0=A.min, op1=A.add))
                    vm = Vm(lambda e, o=kT_o[64:128, cs], sc=scr[64:128, :], rc=rcr[64:128, :]:
                            e.scalar_tensor_tensor(out=o, in0=sc, scalar=1.0, in1=rc, op0=A.min, op1=A.add), inc="V")
                slot_mark[oc % 2] = vm
                P.bank_set([b], ("SC", sc_m))
        qk_done = slot_mark[1 - 1], slot_mark[1]
        tail_pump(4)

        kv_marks = []
        for ts in range(2):
            for nm, is_k in (("WkT", True), ("WvT", False)):
                banks = (4, 5) if is_k else (2, 3)
                P.bank_wait("tensor", *banks)
                if not is_k:
                    P.w("tensor", "V", kv_marks[-1])
                pe_v = None
                for kc in range(8):
                    stat = xT_bf[:, ST * kc + 128 * ts:ST * kc + 128 * (ts + 1)]
                    for hf in range(2):
                        mov = wT[nm][:, D * kc + 512 * hf:D * kc + 512 * (hf + 1)]
                        pe_v = Tm(lambda e, o=pba[banks[hf]], s=stat, m=mov, st_=(kc == 0), sp=(kc == 7):
                                  e.matmul(o[:, 0:512], s, m, start=st_, stop=sp),
                                  inc="PE" if (kc == 7 and hf == 1) else None) or pe_v
                P.w("vector", "PE", pe_v)
                if is_k:
                    for hf in range(2):
                        Vm(lambda e, hf=hf, b=banks[hf]: e.tensor_tensor(out=tmp1[:, 512 * hf:512 * (hf + 1)],
                                                                         in0=pba[b][:, 0:512], in1=bk_bc[:, 512 * hf:512 * (hf + 1)], op=A.add))
                    vv = P.op("vector", lambda e: e.nop(), inc="V", dr=True)
                    P.bank_set(banks, ("V", vv))
                    P.w("scalar", "V", vv)
                    Sm(lambda e: e.activation(out=exp_sc, in_=tmp1, func=AF.Exp))
                    sc_m = Sm(lambda e: e.activation(out=relu_sc, in_=tmp1, func=AF.Relu), inc="SC")
                    P.w("vector", "SC", sc_m)
                    vm = Vm(lambda e, o=kt_sb[:, D * ts:D * (ts + 1)]:
                            e.scalar_tensor_tensor(out=o, in0=exp_sc, scalar=1.0, in1=relu_sc, op0=A.min, op1=A.add), inc="V")
                    kv_marks.append(vm)
                else:
                    for hf in range(2):
                        vm = Vm(lambda e, hf=hf, b=banks[hf], o=v3[:, ts, 8 * hf:8 * hf + 8, 0:64]:
                                e.tensor_tensor(out=o, in0=pba[b][:, 0:512], in1=bv_bc[:, 512 * hf:512 * (hf + 1)], op=A.add),
                                inc="V" if hf == 1 else None)
                    kv_marks.append(vm)
                    P.bank_set(banks, ("V", vm))
            last["vproj"] = P.c["PE"]
        kt_done = kv_marks[2]
        v_done = kv_marks[3]
        tail_pump(2)
        nfin = 1 if (wqq and wqq[0] == 64) else 0
        n4 = max(1, (len(wqq) - nfin) // 4) if wqq else 0
        wq_pump(wqq[:nfin + n4])
        del wqq[:nfin + n4]
        if i == n_stiles - 1 == 7 and stage >= 4:
            P.w("sync", "PE", last["vproj"])
            for _q in ("dQ0", "dQ1", "dQ2", "dQ3"):
                P.w("sync", _q)
            for kc in range(6):
                P.dma("sync", "dWq", W1q_sb[:, DH * kc:DH * (kc + 1)], d_w1qsp[128 * kc:128 * (kc + 1), :])

        for lc in range(2):
            c = 2 * i + lc
            pcols = lambda h: slice(ST * (h // 2) + CH * lc, ST * (h // 2) + CH * (lc + 1))
            qfull = lambda h: qT_sb[:, pcols(h)]
            smask = lambda h: (S_e if h % 2 == 0 else S_o)[:, 65 * (h // 2):65 * (h // 2) + 65]
            kmask = lambda h: (kT_e if h % 2 == 0 else kT_o)[:, pcols(h)]
            kth = lambda h: kt_sb[:, D * lc + HD * h:D * lc + HD * (h + 1)]
            vone = lambda h: v3[:, lc, h, :]

            def spsum(h):
                j = h // 2
                return pba[j // 4][64 * (h % 2):64 * (h % 2) + 64, 65 * (j % 4):65 * (j % 4) + 65]

            mask_marks = []
            for g, banks in ((0, (6, 7)), (1, (2, 3))):
                P.bank_wait("tensor", *banks)
                if c == 0 and g == 1:
                    P.w("tensor", "V", v_done)
                pe_sc = None
                for hh in range(8):
                    h = 8 * g + hh
                    pe_sc = Tm(lambda e, o=pba[banks[hh // 4]], s=kmask(h), m=qfull(h), off=128 * (hh % 4):
                               e.matmul(o[:, off:off + 128], s, m, start=True, stop=True),
                               inc="PE" if hh == 7 else None) or pe_sc
                P.w("vector", "PE", pe_sc)
                mm = None
                for bk_ in range(2):
                    mm = Vm(lambda e, g=g, bk_=bk_, b=banks[bk_]: e.tensor_tensor(
                        out=At_sb[:, CH * 8 * g + 512 * bk_:CH * 8 * g + 512 * (bk_ + 1)],
                        in0=pba[b][:, 0:512], in1=mask4, op=A.mult),
                        inc="V" if bk_ == 1 else None) or mm
                mask_marks.append(mm)
                P.bank_set(banks, ("V", mm))

            P.bank_wait("tensor", 3, 4, 5)
            P.w("tensor", "V", mask_marks[1])
            if c > 0:
                P.w("tensor", "V", last["scopy"])
            P.w("tensor", "V", v_done)
            pe_ch = None
            for h in range(16):
                Ath = At_sb[:, CH * h:CH * (h + 1)]
                Tm(lambda e, o=opsum(h), s=Ath, m=vone(h), st_=True, sp=(c == 0):
                   e.matmul(o, s, m, start=st_, stop=sp, skip_group_check=True))
                if c > 0:
                    Tm(lambda e, o=opsum(h), s=qfull(h), m=smask(h):
                       e.matmul(o, s, m, start=False, stop=True, skip_group_check=True))
            P.bank_wait("tensor", 0, 1)
            for h in range(16):
                first = (c == 0) and (h in (0, 1, 8, 9))
                tp = (0, 64) if h % 2 else None
                pe_ch = Tm(lambda e, o=spsum(h), s=kth(h), m=vone(h), st_=first, tp=tp:
                           e.matmul(o, s, m, start=st_, stop=False, tile_position=tp, skip_group_check=True),
                           inc="PE" if h == 15 else None) or pe_ch

            P.w("vector", "PE", pe_ch)
            Vm(lambda e: e.tensor_copy(out=S_e[0:64, 0:260], in_=pba[0][0:64, 0:260]))
            Vm(lambda e: e.tensor_copy(out=S_e[0:64, 260:520], in_=pba[1][0:64, 0:260]))
            Vm(lambda e: e.tensor_copy(out=S_o[64:128, 0:260], in_=pba[0][64:128, 0:260]))
            sm = Vm(lambda e: e.tensor_copy(out=S_o[64:128, 260:520], in_=pba[1][64:128, 0:260]), inc="V")
            last["scopy"] = sm
            P.bank_set([0, 1], ("V", sm))
            Vm(lambda e: e.tensor_scalar(out=st["denf"][:, 0:7], in0=pba[4][:, 0:455].rearrange("p (s c) -> p s c", c=65)[:, :, 64:65], scalar1=EPS_DEN, scalar2=None, op0=A.add))
            Vm(lambda e: e.tensor_scalar(out=st["denf"][:, 7:14], in0=pba[5][:, 0:455].rearrange("p (s c) -> p s c", c=65)[:, :, 64:65], scalar1=EPS_DEN, scalar2=None, op0=A.add))
            Vm(lambda e: e.tensor_scalar(out=st["denf"][:, 14:16], in0=pba[3][:, 0:130].rearrange("p (s c) -> p s c", c=65)[:, :, 64:65], scalar1=EPS_DEN, scalar2=None, op0=A.add))
            Vm(lambda e: e.reciprocal(out=st["denr"], in_=st["denf"]))
            dr3 = lambda a, b: st["denr"][:, a:b].rearrange("p (h o) -> p h o", o=1).broadcast_to([128, b - a, 64])
            onum = lambda bank, n: pba[bank][:, 0:65 * n].rearrange("p (h c) -> p h c", c=65)[:, :, 0:64]
            attn3 = lambda lc, a, b: attn_sb[:, D * lc + HD * a:D * lc + HD * b].rearrange("p (h c) -> p h c", c=64)
            Vm(lambda e, lc=lc: e.tensor_tensor(out=attn3(lc, 0, 7), in0=onum(4, 7), in1=dr3(0, 7), op=A.mult))
            Vm(lambda e, lc=lc: e.tensor_tensor(out=attn3(lc, 7, 14), in0=onum(5, 7), in1=dr3(7, 14), op=A.mult))
            om = Vm(lambda e, lc=lc: e.tensor_tensor(out=attn3(lc, 14, 16), in0=onum(3, 2), in1=dr3(14, 16), op=A.mult), inc="V")
            last["oevac%d" % lc] = om
            P.bank_set([3, 4, 5], ("V", om))
            if lc == 0:
                tail_pump(2)
                wq_pump(wqq[:n4])
                del wqq[:n4]
            else:
                tail_pump(4)

        P.w("vector", "dC")
        tposm = {}
        pair67 = pairB[3]
        for lc in range(2):
            P.w("tensor", "V", last["oevac%d" % lc])
            for j0 in range(0, 8, 2):
                P.bank_wait("tensor", 6, 7)
                for dj in range(2):
                    src = attn_sb[:, D * lc + 128 * (j0 + dj):D * lc + 128 * (j0 + dj + 1)]
                    pe_t = Tm(lambda e, o=pbfa[6 + dj], s=src: e.transpose(o[:, 0:128], s, ident),
                              inc="PE" if dj == 1 else None)
                P.w("vector", "PE", pe_t)
                src2 = pair67[:, 0:1152].rearrange("p (a c) -> p a c", c=128)[:, 0:9:8, :]
                dst2 = attnT_sb[:, ST * j0 + CH * lc:ST * (j0 + 2)].rearrange("p (a c) -> p a c", c=128)[:, 0:3:2, :] if lc == 0 else \
                       attnT_sb[:, ST * j0:ST * (j0 + 2)].rearrange("p (a c) -> p a c", c=128)[:, 1:4:2, :]
                vm = Vm(lambda e, o=dst2, p=src2: e.tensor_copy(out=o, in_=p), inc="V")
                P.bank_set([6, 7], ("V", vm))
            tposm[lc] = vm

        tail_flush()
        P.w("sync", "V", last.get("r1add", 0))
        P.w("sync", "dSp")
        for ts in range(2):
            P.dma("sync", "dX", x_t[:, D * ts:D * (ts + 1)], dx[t0 + 128 * ts:t0 + 128 * (ts + 1), :])
        last["xs"] = P.c["dX"]
        wq_pump(wqq[:n4])
        del wqq[:n4]
        wo_banks = {0: (4, 5), 1: (2, 3)}
        pe_wo = {}
        for ts in range(2):
            bks = wo_banks[ts]
            P.bank_wait("tensor", *bks)
            P.w("tensor", "V", tposm[ts])
            pw = None
            for cc in range(8):
                stat = attnT_sb[:, ST * cc + 128 * ts:ST * cc + 128 * (ts + 1)]
                for hf in range(2):
                    mov = wT["WoT"][:, D * cc + 512 * hf:D * cc + 512 * (hf + 1)]
                    pw = Tm(lambda e, o=pba[bks[hf]], s=stat, m=mov, st_=(cc == 0), sp=(cc == 7):
                            e.matmul(o[:, 0:512], s, m, start=st_, stop=sp),
                            inc="PE" if (cc == 7 and hf == 1) else None) or pw
            pe_wo[ts] = pw

        rm_ts = {}
        for ts in range(2):
            bks = wo_banks[ts]
            P.w("vector", "PE", pe_wo[ts])
            if ts == 0:
                P.w("vector", "dX", last["xs"])
                P.w("vector", "SC", last.get("tail_apm", 0))
            for hf in range(2):
                Vm(lambda e, hf=hf, b=bks[hf]: e.tensor_tensor(out=tmp1[:, 512 * hf:512 * (hf + 1)],
                                                               in0=pba[b][:, 0:512], in1=bo_bc[:, 512 * hf:512 * (hf + 1)], op=A.add))
            rm = Vm(lambda e, ts=ts: e.tensor_tensor(out=r1_sb[:, D * ts:D * (ts + 1)], in0=tmp1, in1=x_t[:, D * ts:D * (ts + 1)], op=A.add), inc="V")
            rm_ts[ts] = rm
            P.bank_set(list(bks), ("V", rm))
        last["r1add"] = rm_ts[1]

        cols = [(2 * i + ts) % 16 for ts in range(2)]
        r1s_ = [r1_sb[:, D * ts:D * (ts + 1)] for ts in range(2)]
        r1l_ = [r1ln_sb[:, D * ts:D * (ts + 1)] for ts in range(2)]
        tbox = {}

        def t_sums(ts):
            def go():
                c = cols[ts]
                P.w("scalar", "V", rm_ts[ts])
                Sm(lambda e, c=c, r=r1s_[ts], o=r1l_[ts]: e.activation(out=o, in_=r, func=AF.Square, accum_out=st["u"][:, c:c + 1]))
                tbox["sc1", ts] = Sm(lambda e, c=c, r=r1s_[ts], o=r1l_[ts]: e.activation(out=o, in_=r, func=AF.Identity, accum_out=st["m"][:, c:c + 1]), inc="SC")
            return go

        def t_tiny(ts):
            def go():
                c = cols[ts]
                P.w("vector", "SC", tbox["sc1", ts])
                Vm(lambda e, c=c: e.tensor_scalar(out=st["m"][:, c:c + 1], in0=st["m"][:, c:c + 1], scalar1=1.0 / D, scalar2=None, op0=A.mult))
                Vm(lambda e, c=c: e.tensor_tensor(out=st["t0"][:, c:c + 1], in0=st["m"][:, c:c + 1], in1=st["m"][:, c:c + 1], op=A.mult))
                tbox["um", ts] = Vm(lambda e, c=c: e.tensor_scalar(out=st["u"][:, c:c + 1], in0=st["u"][:, c:c + 1], scalar1=1.0 / D,
                                                                  scalar2=st["t0"][:, c:c + 1], op0=A.mult, op1=A.subtract), inc="V")
            return go

        def t_lnexp(ts):
            def go():
                c = cols[ts]
                P.w("scalar", "V", tbox["um", ts])
                Sm(lambda e, c=c: e.activation(out=st["t1"][:, c:c + 1], in_=st["u"][:, c:c + 1], func=AF.Ln, bias=st["eps"][:, 0:1]))
                tbox["scm", ts] = Sm(lambda e, c=c: e.activation(out=st["rstd"][:, c:c + 1], in_=st["t1"][:, c:c + 1], func=AF.Exp, scale=-0.5), inc="SC")
            return go

        def t_apply(ts):
            def go():
                c = cols[ts]
                P.w("vector", "SC", tbox["scm", ts])
                nb = Vm(lambda e, c=c: e.tensor_scalar(out=st["nbm"][:, c:c + 1], in0=st["m"][:, c:c + 1],
                                                       scalar1=st["rstd"][:, c:c + 1], scalar2=-1.0, op0=A.mult, op1=A.mult), inc="V")
                P.w("scalar", "V", nb)
                apm = Sm(lambda e, c=c, r=r1s_[ts], o=r1l_[ts]: e.activation(out=o, in_=r, func=AF.Identity,
                                                                             scale=st["rstd"][:, c:c + 1], bias=st["nbm"][:, c:c + 1]), inc="SC")
                tbox["ap", ts] = apm
                last["tail_apm"] = apm
            return go

        def t_amax(ts):
            def go():
                c = cols[ts]
                P.w("vector", "SC", tbox["ap", ts])
                Vm(lambda e, c=c, r=r1l_[ts]: e.reduce_max(out=st["mx2"][:, c:c + 1], in_=r, axis=mybir.AxisListType.X, apply_absolute_value=True))
                Vm(lambda e, c=c: e.tensor_scalar(out=st["mx2"][:, c:c + 1], in0=st["mx2"][:, c:c + 1], scalar1=1e-5, scalar2=None, op0=A.max))
                Vm(lambda e, c=c: e.tensor_scalar(out=st["colraw"][:, c:c + 1], in0=st["mx2"][:, c:c + 1], scalar1=1.0 / 127.0, scalar2=None, op0=A.mult))
                tbox["sa", ts] = Vm(lambda e, c=c: e.reciprocal(out=st["sa"][:, c:c + 1], in_=st["colraw"][:, c:c + 1]), inc="V")
            return go

        def t_quant(ts):
            def go():
                c = cols[ts]
                P.w("scalar", "V", tbox["sa", ts])
                if ts == 1:
                    P.w("scalar", "V", tbox["xm", 0])
                qm = Sm(lambda e, c=c, r=r1l_[ts]: e.activation(out=xq_i32, in_=r, func=AF.Identity, scale=st["sa"][:, c:c + 1]), inc="SC")
                P.w("vector", "SC", qm)
                if ts == 0:
                    P.w("vector", "dC")
                tbox["xm", ts] = Vm(lambda e, o=xq_bf[:, D * ts:D * (ts + 1)]: e.tensor_copy(out=o, in_=xq_i32), inc="V")

                P.w("sync", "V", tbox["sa", ts])
                P.w("sync", "SC", tbox["ap", ts])
                P.dma("sync", "dSp", d_r1sp[t0 + 128 * ts:t0 + 128 * (ts + 1), :], r1l_[ts])
                P.w("sync", "V", tbox["xm", ts])
                P.dma("sync", "dC", d_xq2[1 + t0 + 128 * ts:1 + t0 + 128 * (ts + 1), :], xq_bf[:, D * ts:D * (ts + 1)])
            return go

        tail_pend.extend([
            t_sums(0), t_sums(1), t_tiny(0), t_tiny(1), t_lnexp(0), t_lnexp(1),
            t_apply(0), t_apply(1), t_amax(0), t_quant(0), t_amax(1), t_quant(1)])

        if dtap:
            tail_flush()
            P.w("scalar", "V", P.c["V"])
            P.w("scalar", "SC", P.c["SC"])
        if "r1ln" in dtap:
            for ts in range(2):
                P.dma("scalar", "dC", dtap["r1ln"][t0 + 128 * ts:t0 + 128 * (ts + 1), :], r1ln_sb[:, D * ts:D * (ts + 1)])
        if "attn" in dtap:
            for ts in range(2):
                P.dma("scalar", "dC", dtap["attn"][t0 + 128 * ts:t0 + 128 * (ts + 1), :], attn_sb[:, D * ts:D * (ts + 1)])
        if "xq" in dtap:
            for ts in range(2):
                P.dma("scalar", "dC", dtap["xq"][t0 + 128 * ts:t0 + 128 * (ts + 1), :], xq_bf[:, D * ts:D * (ts + 1)])
        if i == n_stiles - 1:
            while tail_pend or wqq:
                tail_pump(1)
                wq_pump(wqq[:1])
                del wqq[:1]
        else:
            wq_pump(wqq)
            del wqq[:]
        if i == n_stiles - 1 == 7 and stage >= 4:
            P.w("sync", "V", last["r1add"])
            for _q in ("dQ0", "dQ1", "dQ2", "dQ3"):
                P.w("sync", _q)
            for kc in range(6, 8):
                P.dma("sync", "dWq", W1q_sb[:, DH * kc:DH * (kc + 1)], d_w1qsp[128 * kc:128 * (kc + 1), :])

    def ftile(j):
        f0 = FT * j
        WFl = WF
        xqTb = xqTb2[j % 2]
        colsc_row = colsc_row2[j % 2]

        pf = "dPf%d" % (j % 2)
        r1b_sb = r1b2[j % 2]
        if j > 0:
            P.w("sync", "dOut")
            for ts in range(2):
                P.dma("sync", "dX", r1b_sb[:, D * ts:D * (ts + 1)], d_r1sp[f0 + 128 * ts:f0 + 128 * (ts + 1), :])
        dxm = P.c["dX"]

        if j == 0:
            P.w("tensor", pf, last[pf])
            P.bank_wait("tensor", 0)
            cbm_t = Tm(lambda e, colsc_row=colsc_row: e.matmul(pba[0][:, 0:WFl], sc_stat1, colsc_row, start=True, stop=True), inc="PE")
            P.w("vector", "PE", cbm_t)
            cbm = Vm(lambda e: e.tensor_copy(out=colsc_bc, in_=pba[0][:, 0:WFl]), inc="V")
            P.bank_set([0], ("V", cbm))
        else:
            cbm = last["cbmv"]
        if j + 1 < n_ftiles:
            nf0 = FT * (j + 1)
            npf = "dPf%d" % ((j + 1) % 2)
            P.w("sync", "PE", last.get("w1end", 0))
            for kc in range(8):
                P.dma("sync", npf, xqTb2[(j + 1) % 2][:, XW * kc:XW * (kc + 1)], d_xq2[nf0:nf0 + XW, 128 * kc:128 * (kc + 1)], transpose=True)
            P.dma("sync", npf, colsc_row2[(j + 1) % 2], d_colsp[0:1, nf0:nf0 + WFl])
            last[npf] = P.c[npf]

        silu_marks = {}
        hevac = {}
        if j == 0:
            for cc in range(4):
                last["diag0", cc] = Vm(lambda e, cc=cc: e.tensor_scalar(
                    out=diag0_all[:, 128 * cc:128 * (cc + 1)], in0=ident,
                    scalar1=convw_sb[:, 3 * cc:3 * cc + 1], scalar2=None, op0=A.mult), inc="V")
        macc_m = 0
        for cc in range(32):
            bW1 = cc % 2
            bCv = 2 + cc % 2

            P.bank_wait("tensor", bW1)
            if cc == 0:
                P.w("tensor", "V", cbm)
            pe_w1 = None
            for kc in range(8):
                stat = W1q_sb[:, DH * kc + 128 * cc:DH * kc + 128 * (cc + 1)]
                mov = xqTb[:, XW * kc:XW * kc + WFl]
                pe_w1 = Tm(lambda e, o=pba[bW1], s=stat, m=mov, st_=(kc == 0), sp=(kc == 7):
                           e.matmul(o[:, 0:WFl], s, m, start=st_, stop=sp),
                           inc="PE" if kc == 7 else None) or pe_w1
            if cc == 31:
                last["w1end"] = pe_w1
            P.w("vector", "PE", pe_w1)
            hrow = h_sb[:, WFl * cc:WFl * (cc + 1)]
            hm = Vm(lambda e, o=hrow, p=pba[bW1]: e.tensor_tensor(out=o, in0=p[:, 0:WFl], in1=colsc_bc, op=A.mult), inc="V")
            hevac[cc] = hm
            P.bank_set([bW1], ("V", hm))

            if j == 0 and cc + 4 < 32:
                last["diag0", cc + 4] = Vm(lambda e, cc=cc: e.tensor_scalar(
                    out=diag0_all[:, 128 * (cc + 4):128 * (cc + 5)], in0=ident,
                    scalar1=convw_sb[:, 3 * (cc + 4):3 * (cc + 4) + 1], scalar2=None, op0=A.mult), inc="V")
            P.bank_wait("tensor", bCv)
            P.w("tensor", "V", hm)
            if j == 0:
                P.w("tensor", "V", last["diag0", cc])
            pe_cv = Tm(lambda e, o=pba[bCv], s=diag0_all[:, 128 * cc:128 * (cc + 1)],
                       m=h_sb[:, WFl * cc:WFl * cc + FT]:
                       e.matmul(o[:, 0:FT], s, m, start=True, stop=True), inc="PE")
            if cc > 0:
                P.w("vector", "SC", silu_marks[cc - 1])
                hsp = h_sb[:, WFl * (cc - 1) + 1:WFl * (cc - 1) + 1 + FT]
                if cc == 1:
                    Vm(lambda e, hs=hsp: e.tensor_copy(out=maccP[1], in_=hs))
                    macc_m = Vm(lambda e, hs=hsp: e.tensor_copy(out=maccN[1], in_=hs), inc="V")
                else:
                    Vm(lambda e, hs=hsp, cc=cc: e.tensor_tensor(out=maccP[cc % 2], in0=maccP[1 - cc % 2], in1=hs, op=A.max))
                    macc_m = Vm(lambda e, hs=hsp, cc=cc: e.tensor_tensor(out=maccN[cc % 2], in0=maccN[1 - cc % 2], in1=hs, op=A.min), inc="V")
            P.w("vector", "PE", pe_cv)
            cvm = None
            for tap in (1, 2):
                mov = h_sb[:, WFl * cc + tap:WFl * cc + tap + FT]
                cw = convw_sb[:, 3 * cc + tap:3 * cc + tap + 1]
                cvm = Vm(lambda e, o=pba[bCv], m=mov, cw=cw: e.scalar_tensor_tensor(
                    out=o[:, 0:FT], in0=m, scalar=cw, in1=o[:, 0:FT], op0=A.mult, op1=A.add),
                    inc="V" if tap == 2 else None) or cvm
            P.w("scalar", "V", cvm)
            hs = h_sb[:, WFl * cc + 1:WFl * cc + 1 + FT]
            sm = Sm(lambda e, o=hs, p=pba[bCv], b=convb_sb[:, cc:cc + 1]:
                    e.activation(out=o, in_=p[:, 0:FT], func=AF.Silu, bias=b), inc="SC")
            silu_marks[cc] = sm
            P.bank_set([bCv], ("SC", sm))
            if cc in (2, 5, 8, 11, 14, 17, 20, 23):
                ln2_pump(1)
        P.w("vector", "SC", silu_marks[31])
        hsp = h_sb[:, WFl * 31 + 1:WFl * 31 + 1 + FT]
        Vm(lambda e, hs=hsp: e.tensor_tensor(out=maccP[0], in0=maccP[1], in1=hs, op=A.max))
        Vm(lambda e, hs=hsp: e.tensor_tensor(out=maccN[0], in0=maccN[1], in1=hs, op=A.min))
        Vm(lambda e: e.tensor_scalar(out=maccN[1], in0=maccN[0], scalar1=-1.0, scalar2=None, op0=A.mult))
        mh_m = Vm(lambda e: e.tensor_tensor(out=maccP[1], in0=maccP[0], in1=maccN[1], op=A.max), inc="V")

        P.bank_wait("tensor", 2, 3)
        P.w("tensor", "V", mh_m)
        tpm = {}
        for ts in range(2):
            tpm[ts] = Tm(lambda e, o=pbfa[2 + ts], s=maccP[1][:, 128 * ts:128 * (ts + 1)]: e.transpose(o[:, 0:128], s, ident), inc="PE")
        P.w("vector", "PE", tpm[1])
        vm = None
        for ts in range(2):
            col = (2 * j + ts) % 16
            Vm(lambda e, c=col, ts=ts: e.reduce_max(out=st["mhA"][:, c:c + 1], in_=pbfa[2 + ts][:, 0:128], axis=mybir.AxisListType.X))
            Vm(lambda e, c=col: e.tensor_scalar(out=st["mhA"][:, c:c + 1], in0=st["mhA"][:, c:c + 1], scalar1=1e-5, scalar2=None, op0=A.max))
            Vm(lambda e, c=col: e.reciprocal(out=st["t2"][:, c:c + 1], in_=st["mhA"][:, c:c + 1]))
            Vm(lambda e, c=col: e.tensor_scalar(out=st["s2"][:, c:c + 1], in0=st["t2"][:, c:c + 1], scalar1=127.0, scalar2=None, op0=A.mult))
            vm = Vm(lambda e, c=col: e.tensor_tensor(out=st["ev2"][:, c:c + 1], in0=st["mhA"][:, c:c + 1], in1=nb2_pp, op=A.mult), inc="V") or vm
        P.w("tensor", "V", vm)
        s2tp = None
        for ts in range(2):
            col = (2 * j + ts) % 16
            s2tp = Tm(lambda e, c=col, ts=ts: e.transpose(pba[3][0:1, 128 * ts:128 * (ts + 1)], st["s2"][:, c:c + 1], identf),
                      inc="PE" if ts == 1 else None) or s2tp
        P.w("vector", "PE", s2tp)
        s2rm = Vm(lambda e: e.tensor_copy(out=s2_row, in_=pba[3][0:1, 0:FT]), inc="V")
        P.w("tensor", "V", s2rm)
        s2t = Tm(lambda e: e.matmul(pba[2][:, 0:FT], ones_rowf, s2_row, start=True, stop=True), inc="PE")
        P.w("vector", "PE", s2t)
        s2m = Vm(lambda e: e.tensor_copy(out=s2_bc, in_=pba[2][:, 0:FT]), inc="V")
        P.bank_set([2, 3], ("V", s2m))

        if j + 1 < n_ftiles:
            npf = "dPf%d" % ((j + 1) % 2)
            ncol = colsc_row2[(j + 1) % 2]
            P.w("tensor", npf, last[npf])
            P.bank_wait("tensor", 0)
            cbm_t = Tm(lambda e, c=ncol: e.matmul(pba[0][:, 0:WFl], sc_stat1, c, start=True, stop=True), inc="PE")
            P.w("vector", "PE", cbm_t)
            ncb = Vm(lambda e: e.tensor_copy(out=colsc_bc, in_=pba[0][:, 0:WFl]), inc="V")
            P.bank_set([0], ("V", ncb))
            last["cbmv"] = ncb

        ln2_flush()
        if last.get("ln2sc"):
            P.w("vector", "SC", last["ln2sc"])
        if j == 0:
            P.w("tensor", "dWq2", P.c["dWq2"])
        P.bank_wait("tensor", 4, 5, 6, 7)
        qv_marks = {}
        pe_w2 = None
        for cc in range(32):
            hs = h_sb[:, WFl * cc + 1:WFl * cc + 1 + FT]
            P.w("vector", "SC", qv_marks.get(("sc", cc - 2), 0))
            qs = q_scr[:, FT * (cc % 2):FT * (cc % 2 + 1)]
            hq = hq_i32[:, FT * (cc % 2):FT * (cc % 2 + 1)]
            vt = Vm(lambda e, hs=hs, qs=qs: e.tensor_tensor(out=qs, in0=hs, in1=s2_bc, op=A.mult), inc="V")
            P.w("scalar", "V", vt)
            scm = Sm(lambda e, qs=qs, hq=hq: e.activation(out=hq, in_=qs, func=AF.Identity), inc="SC")
            qv_marks[("sc", cc)] = scm
            P.w("vector", "SC", scm)
            cm = Vm(lambda e, hs=hs, hq=hq: e.tensor_copy(out=hs, in_=hq), inc="V")
            qv_marks[cc] = cm
            P.w("tensor", "V", cm)
            for ts in range(2):
                stat = h_sb[:, WFl * cc + 1 + 128 * ts:WFl * cc + 1 + 128 * (ts + 1)]
                for hf in range(2):
                    bb = 4 + 2 * ts + hf
                    mov = W2q_sb[:, D * cc + 512 * hf:D * cc + 512 * (hf + 1)]
                    pe_w2 = Tm(lambda e, o=pba[bb], s=stat, m=mov, st_=(cc == 0), sp=(cc == 31):
                               e.matmul(o[:, 0:512], s, m, start=st_, stop=sp),
                               inc="PE" if (cc == 31 and ts == 1 and hf == 1) else None) or pe_w2
        last["w2"] = pe_w2

        box = {"lnm": None, "apm": None}

        def _evac(ts, col):
            def go():
                if ts == 0:
                    P.w("vector", "PE", pe_w2)
                    P.w("vector", "dX")
                else:
                    P.w("vector", "SC", box["apm"])
                lnm = None
                for hf in range(2):
                    bb = 4 + 2 * ts + hf
                    lnm = Vm(lambda e, c=col, bb=bb, hf=hf, ts=ts: e.scalar_tensor_tensor(
                        out=r2_sb[:, 512 * hf:512 * (hf + 1)], in0=pba[bb][:, 0:512],
                        scalar=st["ev2"][:, c:c + 1], in1=r1b_sb[:, D * ts + 512 * hf:D * ts + 512 * (hf + 1)],
                        op0=A.mult, op1=A.add), inc="V" if hf == 1 else None) or lnm
                box["lnm"] = lnm
            return go

        def _sums(ts, col):
            def go():
                P.w("scalar", "V", box["lnm"])
                Sm(lambda e, c=col: e.activation(out=ln2_junk, in_=r2_sb, func=AF.Square, accum_out=st["u2"][:, c:c + 1]))
                sc1 = Sm(lambda e, c=col: e.activation(out=ln2_junk, in_=r2_sb, func=AF.Identity, accum_out=st["m2"][:, c:c + 1]), inc="SC")
                box["sc1"] = sc1
            return go

        def _stat(ts, col):
            def go():
                P.w("vector", "SC", box["sc1"])
                Vm(lambda e, c=col: e.tensor_scalar(out=st["m2"][:, c:c + 1], in0=st["m2"][:, c:c + 1], scalar1=1.0 / D, scalar2=None, op0=A.mult))
                Vm(lambda e, c=col: e.tensor_tensor(out=st["t3"][:, c:c + 1], in0=st["m2"][:, c:c + 1], in1=st["m2"][:, c:c + 1], op=A.mult))
                um = Vm(lambda e, c=col: e.tensor_scalar(out=st["u2"][:, c:c + 1], in0=st["u2"][:, c:c + 1], scalar1=1.0 / D,
                                                         scalar2=st["t3"][:, c:c + 1], op0=A.mult, op1=A.subtract), inc="V")
                P.w("scalar", "V", um)
                Sm(lambda e, c=col: e.activation(out=st["t3"][:, c:c + 1], in_=st["u2"][:, c:c + 1], func=AF.Ln, bias=st["eps"][:, 0:1]))
                scm = Sm(lambda e, c=col: e.activation(out=st["rstd2"][:, c:c + 1], in_=st["t3"][:, c:c + 1], func=AF.Exp, scale=-0.5), inc="SC")
                box["scm"] = scm
            return go

        def _apply(ts, col):
            def go():
                P.w("vector", "SC", box["scm"])
                nb = Vm(lambda e, c=col: e.tensor_scalar(out=st["nbm"][:, c:c + 1], in0=st["m2"][:, c:c + 1],
                                                         scalar1=st["rstd2"][:, c:c + 1], scalar2=-1.0, op0=A.mult, op1=A.mult), inc="V")
                P.w("scalar", "V", nb)
                apm = Sm(lambda e, c=col, ts=ts: e.activation(out=r1b_sb[:, D * ts:D * (ts + 1)], in_=r2_sb, func=AF.Identity,
                                                              scale=st["rstd2"][:, c:c + 1], bias=st["nbm"][:, c:c + 1]), inc="SC")
                box["apm"] = apm
                P.w("sync", "SC", apm)
                P.dma("sync", "dOut", dout[f0 + 128 * ts:f0 + 128 * (ts + 1), :], r1b_sb[:, D * ts:D * (ts + 1)])
                if ts == 1:
                    last["ln2"] = box["lnm"]
                    last["ln2sc"] = apm
                    P.bank_set([4, 5, 6, 7], ("V", box["lnm"]))
            return go

        for ts in range(2):
            col = (2 * j + ts) % 16
            ln2_pend.append(_evac(ts, col))
            ln2_pend.append(_sums(ts, col))
            ln2_pend.append(_stat(ts, col))
            ln2_pend.append(_apply(ts, col))

    prologue()
    if stage >= 4 and n_stiles == 8:
        quotas = [12, 24, 28, 17, 15, 15, 14, 4]
        steps = list(range(129))
        sched = []
        p = 0
        for q in quotas:
            sched.append(steps[p:p + q])
            p += q
        while wq_ld[0] < WQ_AHEAD:
            wq_issue_one()
    else:
        sched = [[] for _ in range(n_stiles)]
    for i in range(n_stiles):
        stile(i, wqjobs=sched[i])

    if stage >= 4:
        tail_flush()
        P.bank_wait("tensor", 0, 1)
        P.w("tensor", "V", P.c["V"])
        cr_t = Tm(lambda e: e.transpose(pairA[0][0:16, 0:128], st["colraw"], identf), inc="PE")
        P.w("vector", "PE", cr_t)
        crv = Vm(lambda e: e.tensor_copy(out=colrow_sb, in_=pairA[0][0:16, 0:128]), inc="V")
        P.bank_set([0, 1], ("V", crv))
        P.w("sync", "V", crv)
        for i8 in range(8):
            for ts8 in range(2):
                seg = (2 * i8 + ts8) % 16
                P.dma("sync", "dSp", d_colsp[0:1, 1 + 256 * i8 + 128 * ts8:1 + 256 * i8 + 128 * (ts8 + 1)],
                      colrow_sb[seg:seg + 1, 0:128])
        P.w("sync", "V", P.c["V"])
        P.w("sync", "dC")
        P.w("sync", "dSp")
        for kc in range(8):
            P.dma("sync", "dPf0", xqTb2[0][:, XW * kc:XW * (kc + 1)], d_xq2[0:XW, 128 * kc:128 * (kc + 1)], transpose=True)
        P.dma("sync", "dPf0", colsc_row2[0], d_colsp[0:1, 0:WF])
        last["dPf0"] = P.c["dPf0"]
        for ts in range(2):
            P.dma("sync", "dX", r1b2[0][:, D * ts:D * (ts + 1)], d_r1sp[128 * ts:128 * (ts + 1), :])
        P.barrier("AB")
        P.w("sync", "dSp")
        P.w("sync", "dC")
        for _q in ("dQ0", "dQ1", "dQ2", "dQ3"):
            P.w("sync", _q)
        if n_stiles != 8:
            for kc in range(8):
                P.dma("sync", "dWq", W1q_sb[:, DH * kc:DH * (kc + 1)], d_w1qsp[128 * kc:128 * (kc + 1), :])
        P.w("scalar", "dSp")
        P.w("scalar", "dC")
        for cc in range(32):
            eng = "sync" if cc % 2 == 0 else "scalar"
            P.dma(eng, "dWq2", W2q_sb[:, D * cc:D * (cc + 1)], d_w2qsp[128 * cc:128 * (cc + 1), :])
        P.w("tensor", "dWq", P.c["dWq"])
        for j in range(n_ftiles):
            ftile(j)
        ln2_flush()

    for _sm in ("dS", "dC", "dG", "dGw", "dX", "dW0", "dW1", "dW2", "dW3", "dW4", "dW5", "dQ0", "dQ1", "dQ2", "dQ3", "dH", "dWq", "dWq2", "dSp", "dOut", "dPf0", "dPf1"):
        P.w("sync", _sm)

    import contextlib
    with contextlib.ExitStack() as es:
        block = es.enter_context(nc.Block())

        @block.sync
        def _(e):
            for fn in P.st["sync"]:
                fn(e)

        @block.gpsimd
        def _(e):
            for fn in P.st["gpsimd"]:
                fn(e)

        @block.scalar
        def _(e):
            for fn in P.st["scalar"]:
                fn(e)

        @block.vector
        def _(e):
            for fn in P.st["vector"]:
                fn(e)

        @block.tensor
        def _(e):
            for fn in P.st["tensor"]:
                fn(e)

    return nc


def host_inputs(inp, core):
    f = lambda a: np.ascontiguousarray(np.asarray(a, dtype=np.float32))
    x = f(inp["L0_attn_x"])[core]
    return {
        "x": x,
        "xT": np.ascontiguousarray(x.T),
        "WqT": np.ascontiguousarray(f(inp["L0_attn_Wq"]).T),
        "WkT": np.ascontiguousarray(f(inp["L0_attn_Wk"]).T),
        "WvT": np.ascontiguousarray(f(inp["L0_attn_Wv"]).T),
        "WoT": np.ascontiguousarray(f(inp["L0_attn_Wo"]).T),
        "bq": f(inp["L0_attn_bq"]), "bk": f(inp["L0_attn_bk"]),
        "bv": f(inp["L0_attn_bv"]), "bo": f(inp["L0_attn_bo"]),
        "W1T": np.ascontiguousarray(f(inp["L0_ffn_w1"]).T),
        "W2T": np.ascontiguousarray(f(inp["L0_ffn_w2"]).T),
        "convw": np.ascontiguousarray(f(inp["L0_ffn_conv_w"])[:, 0, :].reshape(32, 128, 3).transpose(1, 0, 2).reshape(128, 96)),
        "convb": np.ascontiguousarray(f(inp["L0_ffn_conv_b"]).reshape(32, 128).T),
    }
def kernel(**inputs):
    nc = build_program(stage=4, taps=(), n_stiles=NST, n_ftiles=NFT)
    in_maps = [host_inputs(inputs, b) for b in range(8)]
    res = run_bass_kernel_spmd(nc, in_maps, core_ids=list(range(8)))
    out = np.stack([np.asarray(res.results[b]["out"], dtype=np.float32) for b in range(8)], axis=0)
    return out


# version 29 (final)
# speedup vs baseline: 1.0038x; 1.0038x over previous
import numpy as np
import concourse.bass as bass
import concourse.mybir as mybir
from concourse.alu_op_type import AluOpType as A
from concourse.bass_utils import run_bass_kernel_spmd

F32 = mybir.dt.float32
BF16 = mybir.dt.bfloat16
I32 = mybir.dt.int32
AF = mybir.ActivationFunctionType

D, SEQ, H, HD, DH = 1024, 2048, 16, 64, 4096
ST = 256
NST = SEQ // ST
FT = 256
NFT = SEQ // FT
CH = 128
EPS_DEN = 1e-6
LN_EPS = 1e-5

ENG = ("sync", "gpsimd", "scalar", "vector", "tensor")


class Sch:
    def __init__(self, nc):
        self.nc = nc
        self.st = {e: [] for e in ENG}
        self.c = {}
        self.sems = {}
        self.bankdep = {}

    def S(self, name):
        if name not in self.sems:
            self.sems[name] = self.nc.alloc_semaphore(name)
            self.c[name] = 0
        return self.sems[name]

    def do(self, eng, fn):
        self.st[eng].append(fn)

    def w(self, eng, sem, val=None):
        sm = self.S(sem)
        v = self.c.get(sem, 0) if val is None else val
        if v > 0:
            self.do(eng, lambda e, sm=sm, v=v: e.wait_ge(sm, v))
        return v

    def op(self, eng, build, inc=None, dr=False):
        if dr:
            self.do(eng, lambda e: e.drain())
        if inc is not None:
            sm = self.S(inc)
            self.c[inc] += 1
            self.do(eng, lambda e, b=build, sm=sm: b(e).then_inc(sm, 1))
            return self.c[inc]
        self.do(eng, build)
        return None

    def dma(self, eng, sem, out, in_, slow=False, transpose=False):
        sm = self.S(sem)
        self.c[sem] += 16
        self.do(eng, lambda e, o=out, i=in_, sm=sm, sl=slow, tp=transpose:
                e.dma_start(out=o, in_=i, transpose=tp, allow_slow_non_contiguous=sl).then_inc(sm, 16))
        return self.c[sem]

    def bank_wait(self, eng, *banks):
        merged = {}
        for b in banks:
            for sem, val in self.bankdep.get(b, ()):
                merged[sem] = max(merged.get(sem, 0), val)
        for sem, val in merged.items():
            self.w(eng, sem, val)

    def bank_set(self, banks, *marks):
        for b in banks:
            self.bankdep[b] = [m for m in marks if m[1]]

    def barrier(self, tag):
        marks = {}
        sems = {"sync": "BSY", "gpsimd": "BG", "scalar": "BSC", "vector": "BV", "tensor": "BPE"}
        for eng, sm in sems.items():
            marks[eng] = self.op(eng, lambda e: e.nop(), inc=sm, dr=True)
        for eng in ENG:
            for oeng, sm in sems.items():
                if oeng != eng:
                    self.w(eng, sm, marks[oeng])


class SBAlloc:
    BASE = 24576
    TOP = 229376

    def __init__(self, nc):
        self.nc = nc
        self.off = self.BASE
        self.marks = {}

    def mark(self, name):
        self.marks[name] = self.off

    def reset_to(self, name):
        self.off = self.marks[name]

    def al(self, name, shape, dt):
        per_part = int(np.prod(shape[1:])) * mybir.dt.size(dt)
        per_part = (per_part + 31) // 32 * 32
        assert self.off + per_part <= self.TOP, f"SBUF overflow at {name}: {self.off}+{per_part}"
        h = self.nc.alloc_sbuf_tensor_at(name, list(shape), dt, offset=self.off)
        self.off += per_part
        return h


def build_program(stage=4, taps=(), n_stiles=NST, n_ftiles=NFT, sub=99):
    nc = bass.Bass("TRN2", target_bir_lowering=False)
    P = Sch(nc)
    for _sm in ("dS", "dG", "dGw", "dC", "dX", "dW0", "dW1", "dW2", "dW3", "dW4", "dW5", "dQ0", "dQ1", "dQ2", "dQ3", "dH", "dWq", "dWq2", "dSp", "dOut", "dPf0", "dPf1", "PE", "V", "SC", "G"):
        P.S(_sm)
    sb = SBAlloc(nc)

    dx = nc.dram_tensor("x", [SEQ, D], F32, kind="ExternalInput").ap()
    dxT = nc.dram_tensor("xT", [D, SEQ], F32, kind="ExternalInput").ap()
    dW = {nm: nc.dram_tensor(nm, [D, D], F32, kind="ExternalInput").ap()
          for nm in ("WqT", "WkT", "WvT", "WoT")}
    db = {nm: nc.dram_tensor(nm, [D], F32, kind="ExternalInput").ap()
          for nm in ("bq", "bk", "bv", "bo")}
    dW1T = nc.dram_tensor("W1T", [D, DH], F32, kind="ExternalInput").ap()
    dW2T = nc.dram_tensor("W2T", [DH, D], F32, kind="ExternalInput").ap()
    dconvw = nc.dram_tensor("convw", [128, 96], F32, kind="ExternalInput").ap()
    dconvb = nc.dram_tensor("convb", [128, 32], F32, kind="ExternalInput").ap()
    dout = nc.dram_tensor("out", [SEQ, D], F32, kind="ExternalOutput").ap()
    d_r1sp = nc.dram_tensor("r1_spill", [SEQ, D], F32, kind="Internal").ap()
    d_xq2 = nc.dram_tensor("xq_spill_tm", [SEQ + 16, D], BF16, kind="Internal").ap()
    d_colsp = nc.dram_tensor("colsc_spill", [1, SEQ + 2], F32, kind="Internal").ap()
    d_w1qsp = nc.dram_tensor("w1q_spill", [D, DH], BF16, kind="Internal").ap()
    d_w2qsp = nc.dram_tensor("w2q_spill", [DH, D], BF16, kind="Internal").ap()
    dtap = {}
    for t in taps:
        shp = {
            "qT": ([D, SEQ], BF16), "kT": ([D, SEQ], BF16),
            "kt": ([SEQ, D], BF16), "v": ([SEQ, D], BF16),
            "attn": ([SEQ, D], BF16), "r1ln": ([SEQ, D], F32),
            "xq": ([SEQ, D], BF16), "wq": ([D, DH], BF16),
            "hsil": ([128, 32 * (FT + 2)], BF16),
            "hq": ([128, 32 * (FT + 2)], BF16),
        }[t]
        dtap[t] = nc.dram_tensor("tap_" + t, list(shp[0]), shp[1], kind="ExternalOutput").ap()

    ident = sb.al("ident", [128, 128], BF16).ap()
    identf = sb.al("identf", [128, 128], F32).ap()
    ones_col = sb.al("ones_col", [128, 1], BF16).ap()
    ones_rowf = sb.al("ones_rowf", [1, 128], F32).ap()
    ones_colf = sb.al("ones_colf", [128, 1], F32).ap()
    zeros_bf = sb.al("zeros_bf", [128, 2], BF16).ap()
    zeros_f = sb.al("zeros_f", [128, 2], F32).ap()
    convw_sb = sb.al("convw_sb", [128, 96], F32).ap()
    convb_sb = sb.al("convb_sb", [128, 32], F32).ap()
    stn = ("m", "u", "rstd", "mx2", "sa", "colraw", "t0", "t1", "denf", "denr",
           "mhA", "s2", "ev2", "m2", "u2", "rstd2", "t2", "t3", "nbm", "eps")
    st = {k: sb.al("st_" + k, [128, 16], F32).ap() for k in stn}
    wabs_acc = sb.al("wabs_acc", [128, 64], F32).ap()
    wsc = sb.al("wsc", [1, 12], F32).ap()
    sc_stat1 = sb.al("sc_stat1", [1, 128], F32).ap()
    colrow_sb = sb.al("colrow_sb", [16, 128], F32).ap()
    nb2_pp = sb.al("nb2_pp", [128, 1], F32).ap()
    thr_pp = sb.al("thr_pp", [128, 4], F32).ap()
    sb.mark("phase")

    wT = {nm: sb.al(nm + "_bf", [128, 8 * D], BF16).ap() for nm in ("WqT", "WkT", "WvT", "WoT")}
    mask4 = sb.al("mask4", [128, 512], F32).ap()
    bq_pp = sb.al("bq_pp", [128, 8], F32).ap()
    bk_pp = sb.al("bk_pp", [128, 8], F32).ap()
    brow_t = sb.al("brow_t", [65, D], F32)
    brow = {nm: brow_t.ap()[32 * i:32 * i + 1, :] for i, nm in enumerate(("bk", "bv", "bo"))}
    ones3 = sb.al("ones3", [65, 128], F32).ap()
    bk_bc = sb.al("bk_bc", [128, D], F32).ap()
    bv_bc = sb.al("bv_bc", [128, D], F32).ap()
    bo_bc = sb.al("bo_bc", [128, D], F32).ap()
    x_t = sb.al("x_t", [128, 2 * D], F32).ap()
    r1ln_sb = x_t
    xT_bf = sb.al("xT_bf", [128, 8 * ST], BF16).ap()
    qT_sb = sb.al("qT_sb", [128, 8 * ST], BF16).ap()
    S_e = sb.al("S_e", [128, 8 * 65], BF16).ap()
    S_o = sb.al("S_o", [128, 8 * 65], BF16).ap()
    kT_e = sb.al("kT_e", [128, 8 * ST], BF16).ap()
    kT_o = sb.al("kT_o", [128, 8 * ST], BF16).ap()
    kt_sb = sb.al("kt_sb", [128, 2 * D], BF16).ap()
    v_sb = sb.al("v_sb", [128, 2 * 16 * 65], BF16).ap()
    At_sb = sb.al("At_sb", [128, 16 * CH], BF16).ap()
    attn_sb = sb.al("attn_sb", [128, 2 * D], BF16).ap()
    attnT_sb = sb.al("attnT_sb", [128, 8 * ST], BF16).ap()
    xqT_bf = attnT_sb
    r1_sb = sb.al("r1_sb", [128, 2 * D], F32).ap()
    xq_i32 = sb.al("xq_i32", [128, D], I32).ap()
    xq_bf = sb.al("xq_bf", [128, 2 * D], BF16).ap()
    exp_sc = sb.al("exp_sc", [128, D], F32).ap()
    relu_sc = sb.al("relu_sc", [128, D], F32).ap()
    tmp1 = sb.al("tmp1", [128, D], F32).ap()
    wfp2 = [sb.al(f"wfp{i}", [128, 1024], F32).ap() for i in range(5)]
    wq_a = sb.al("wq_a", [128, 1024], BF16).ap()
    wab_o = wq_a
    wq_o = [sb.al(f"wq_o{i}", [128, 1024], BF16).ap() for i in range(4)]
    A_END = sb.off

    sb.reset_to("phase")
    WF = FT + 2
    W1q_sb = sb.al("W1q_sb", [128, 8 * DH], BF16).ap()
    W2q_sb = sb.al("W2q_sb", [128, 32 * D], BF16).ap()
    h_sb_h = sb.al("h_sb", [128, 32 * WF], BF16)
    h_sb = h_sb_h.ap()
    r2sq_scr = h_sb_h.bitcast(F32).ap()[:, 0:D]
    XW = FT + 16
    xqTb2 = [sb.al(f"xqTb{i}", [128, 8 * XW], BF16).ap() for i in range(2)]
    r1b2 = [sb.al(f"r1b_{i}", [128, 2 * D], F32).ap() for i in range(2)]
    scrU_h = sb.al("scrU", [128, D], F32)
    r2_sb = scrU_h.ap()
    q_scr = scrU_h.ap()[:, 0:2 * FT]
    hq_i32 = scrU_h.bitcast(I32).ap()[:, 2 * FT:4 * FT]
    colsc_row2 = [sb.al(f"colsc_row{i}", [1, WF], F32).ap() for i in range(2)]
    colsc_bc = sb.al("colsc_bc", [128, WF], F32).ap()
    s2_row = sb.al("s2_row", [1, FT], F32).ap()
    s2_bc = sb.al("s2_bc", [128, FT], F32).ap()
    maccP = [sb.al(f"maccP{i}", [128, FT], BF16).ap() for i in range(2)]
    maccN = [sb.al(f"maccN{i}", [128, FT], BF16).ap() for i in range(2)]
    diag0_all = sb.al("diag0_all", [128, 32 * 128], BF16).ap()
    ln2_junk = sb.al("ln2_junk", [128, D], F32).ap()
    B_END = sb.off

    ppair = [nc.alloc_psum_tensor(f"pp{i}", [128, 1024], F32) for i in range(4)]
    pairA = [p.ap() for p in ppair]
    pairB = [p.bitcast(BF16).ap() for p in ppair]
    pba = [pairA[i // 2][:, 512 * (i % 2):512 * (i % 2 + 1)] for i in range(8)]
    pbfa = [pairB[i // 2][:, 1024 * (i % 2):1024 * (i % 2 + 1)] for i in range(8)]

    Tm = lambda b, inc=None: P.op("tensor", b, inc)
    Vm = lambda b, inc=None: P.op("vector", b, inc, dr=True)
    Sm = lambda b, inc=None: P.op("scalar", b, inc, dr=True)
    Gm = lambda b, inc=None: P.op("gpsimd", b, inc, dr=True)

    def prologue():
        import os as _os
        if _os.environ.get("DK_NOCONST"):
            Gm(lambda e: e.memset(ident, 0.0), inc="G")
        else:
            _prologue_consts()
        if not _os.environ.get("DK_NOWL"):
            _prologue_wl()
        return

    def _prologue_consts():
        import os as _os
        Gm(lambda e: e.memset(ident, 0.0))
        Gm(lambda e: e.affine_select(out=ident, in_=ident, compare_op=A.not_equal,
                                     fill=1.0, base=0, pattern=[[-1, 128]], channel_multiplier=1))
        Gm(lambda e: e.memset(identf, 0.0))
        Gm(lambda e: e.affine_select(out=identf, in_=identf, compare_op=A.not_equal,
                                     fill=1.0, base=0, pattern=[[-1, 128]], channel_multiplier=1))
        Gm(lambda e: e.memset(mask4, 1.0))
        for j in range(4):
            sl = mask4[:, 128 * j:128 * (j + 1)]
            Gm(lambda e, sl=sl: e.affine_select(out=sl, in_=sl, compare_op=A.is_ge,
                                                fill=0.0, base=0, pattern=[[1, 128]], channel_multiplier=-1))
        Gm(lambda e: e.memset(ones_col, 1.0))
        Gm(lambda e: e.memset(ones_rowf, 1.0))
        Gm(lambda e: e.memset(ones_colf, 1.0))
        Gm(lambda e: e.memset(st["eps"][:, 0:1], LN_EPS))
        Gm(lambda e: e.memset(ones3, 1.0))
        Gm(lambda e: e.memset(zeros_bf, 0.0))
        Gm(lambda e: e.memset(S_e, 0.0))
        Gm(lambda e: e.memset(S_o, 0.0))
        Gm(lambda e: e.memset(kT_e, 0.0))
        Gm(lambda e: e.memset(kT_o, 0.0))
        Gm(lambda e: e.memset(v_sb.rearrange("p (l h c) -> p l h c", l=2, c=65)[:, :, :, 64:65], 1.0))
        Gm(lambda e: e.memset(zeros_f, 0.0), inc="G")

    def _prologue_wl():
        import os as _os
        gp_const = P.c["G"]
        for nm in ("WqT", "WkT", "WvT", "WoT"):
            for kc in range(8):
                P.dma("gpsimd", "dGw", wT[nm][:, D * kc:D * (kc + 1)], dW[nm][128 * kc:128 * (kc + 1), :])
        for nm, dst in (("bq", bq_pp), ("bk", bk_pp)):
            for oc in range(8):
                P.dma("gpsimd", "dGw", dst[:, oc:oc + 1], db[nm][128 * oc:128 * (oc + 1)])
        for nm in ("bk", "bv", "bo"):
            P.dma("gpsimd", "dGw", brow[nm], db[nm][:])
        P.dma("gpsimd", "dGw", convw_sb, dconvw)
        P.dma("gpsimd", "dGw", convb_sb, dconvb)
        dg_pro = P.c["dGw"]

        P.w("scalar", "G", gp_const)
        for kc in range(0 if _os.environ.get("DK_NOEDGE") else 8):
            pass
        P.dma("scalar", "dC", d_colsp[0:1, 0:1], zeros_f[0:1, 0:1])
        P.dma("scalar", "dC", d_colsp[0:1, SEQ + 1:SEQ + 2], zeros_f[0:1, 0:1])
        Gm(lambda e: e.memset(xq_bf[:, 0:D], 0.0), inc="G")
        P.w("scalar", "G", P.c["G"])
        P.dma("scalar", "dC", d_xq2[0:1, :], xq_bf[0:1, 0:D])
        P.dma("scalar", "dC", d_xq2[SEQ + 1:SEQ + 16, :], xq_bf[0:15, 0:D])

        P.w("tensor", "dGw", dg_pro)
        P.w("tensor", "G", gp_const)
        if _os.environ.get("DK_NOBC"):
            return
        for i, nm in enumerate(("bk", "bv", "bo")):
            for hf in range(2):
                Tm(lambda e, b=pba[2 * i + hf], s=brow[nm], one=ones3[32 * i:32 * i + 1, :], hf=hf:
                   e.matmul(b[:, 0:512], one, s[:, 512 * hf:512 * (hf + 1)], start=True, stop=True),
                   inc="PE" if (i == 2 and hf == 1) else None)
        P.w("vector", "PE", P.c["PE"])
        for i, dst in enumerate((bk_bc, bv_bc, bo_bc)):
            for hf in range(2):
                Vm(lambda e, d=dst, b=pba[2 * i + hf], hf=hf:
                   e.tensor_copy(out=d[:, 512 * hf:512 * (hf + 1)], in_=b[:, 0:512]),
                   inc="V" if (i == 2 and hf == 1) else None)
        P.w("tensor", "V", P.c["V"])

    WJOBS = [("W1", kc, hf) for kc in range(8) for hf in range(4)] + [("W2", kc, 0) for kc in range(32)]
    NWR = 5
    WQ_AHEAD = 4
    wq_ring = [("SC", 0)] * NWR
    wq_dv = {}
    wq_ld = [0]
    wq_osp = [0, 0, 0, 0]

    def wq_src(idx):
        which, kc, hf = WJOBS[idx]
        if which == "W1":
            return dW1T[128 * kc:128 * (kc + 1), 1024 * hf:1024 * (hf + 1)]
        return dW2T[128 * kc:128 * (kc + 1), :]

    def wq_issue_one():
        L = wq_ld[0]
        slot = L % NWR
        eng, mk = wq_ring[slot]
        if mk:
            P.w("sync", eng, mk)
        wq_dv[L] = P.dma("sync", "dW%d" % slot, wfp2[slot], wq_src(L if L < 64 else L - 64))
        wq_ld[0] = L + 1

    def wq_pump(comps):
        for c in comps:
            ld_needed = c if c < 64 else c - 1
            while wq_ld[0] < min(ld_needed + 1 + WQ_AHEAD, 128):
                wq_issue_one()
            if c < 64:
                wq_pass1_job(c)
            elif c == 64:
                wq_finalize()
            else:
                wq_pass2_job(c - 65)

    def wq_pass1_job(idx):
        slot = idx % NWR
        buf = wfp2[slot]
        P.w("scalar", "dW%d" % slot, wq_dv[idx])
        m = Sm(lambda e, col=idx, buf=buf: e.activation(out=wab_o[:, 0:1024], in_=buf,
                                                        func=AF.Abs, accum_out=wabs_acc[:, col:col + 1]), inc="SC")
        wq_ring[slot] = ("SC", m)

    def wq_finalize():
        P.w("vector", "SC", P.c["SC"])
        Vm(lambda e: e.reduce_sum(out=st["t0"][:, 0:1], in_=wabs_acc[:, 0:32], axis=mybir.AxisListType.X))
        fv = Vm(lambda e: e.reduce_sum(out=st["t0"][:, 1:2], in_=wabs_acc[:, 32:64], axis=mybir.AxisListType.X), inc="V")
        P.bank_wait("tensor", 4)
        P.w("tensor", "V", fv)
        fm = Tm(lambda e: e.matmul(pba[4][0:1, 0:2], ones_colf, st["t0"][:, 0:2], start=True, stop=True), inc="PE")
        P.w("vector", "PE", fm)
        Vm(lambda e: e.tensor_scalar(out=wsc[0:1, 0:2], in0=pba[4][0:1, 0:2], scalar1=1.0 / (D * DH), scalar2=None, op0=A.mult))
        Vm(lambda e: e.tensor_scalar(out=wsc[0:1, 2:4], in0=wsc[0:1, 0:2], scalar1=1e-5, scalar2=None, op0=A.max))
        Vm(lambda e: e.tensor_scalar(out=wsc[0:1, 4:5], in0=wsc[0:1, 2:3], scalar1=0.5, scalar2=None, op0=A.mult))
        Vm(lambda e: e.tensor_scalar(out=wsc[0:1, 5:6], in0=wsc[0:1, 2:3], scalar1=-0.5, scalar2=None, op0=A.mult))
        Vm(lambda e: e.tensor_scalar(out=wsc[0:1, 6:7], in0=wsc[0:1, 3:4], scalar1=0.5, scalar2=None, op0=A.mult))
        Vm(lambda e: e.tensor_scalar(out=wsc[0:1, 7:8], in0=wsc[0:1, 3:4], scalar1=-0.5, scalar2=None, op0=A.mult))
        Vm(lambda e: e.tensor_scalar(out=wsc[0:1, 8:10], in0=wsc[0:1, 2:4], scalar1=-1.0 / 127.0, scalar2=None, op0=A.mult))
        Vm(lambda e: e.tensor_scalar(out=wsc[0:1, 10:11], in0=wsc[0:1, 2:3], scalar1=-1.0, scalar2=None, op0=A.mult))
        fv = Vm(lambda e: e.tensor_scalar(out=sc_stat1, in0=ones_rowf, scalar1=wsc[0:1, 10:11], scalar2=None, op0=A.mult), inc="V")
        P.w("tensor", "V", fv)
        fm = Tm(lambda e: e.matmul(pba[4][:, 0:6], ones_rowf, wsc[0:1, 4:10], start=True, stop=True), inc="PE")
        P.w("vector", "PE", fm)
        Vm(lambda e: e.tensor_copy(out=thr_pp, in_=pba[4][:, 0:4]))
        fv = Vm(lambda e: e.tensor_copy(out=nb2_pp, in_=pba[4][:, 5:6]), inc="V")
        P.bank_set([4], ("V", fv))

    def wq_pass2_job(idx):
        which, kc, hf = WJOBS[idx]
        dst = (d_w1qsp[128 * kc:128 * (kc + 1), 1024 * hf:1024 * (hf + 1)] if which == "W1"
               else d_w2qsp[128 * kc:128 * (kc + 1), :])
        thr = thr_pp[:, 0:1] if which == "W1" else thr_pp[:, 2:3]
        nthr = thr_pp[:, 1:2] if which == "W1" else thr_pp[:, 3:4]
        slot = (64 + idx) % NWR
        buf = wfp2[slot]
        oslot = idx % 4
        obuf = wq_o[oslot]
        if idx % 2 == 1:
            P.w("scalar", "dW%d" % slot, wq_dv[64 + idx])
            P.w("scalar", "dQ%d" % oslot, wq_osp[oslot])
            Sm(lambda e, buf=buf, ob=obuf: e.activation(out=ob[:, 0:1024], in_=buf, func=AF.Sign, scale=-1.0))
            s2 = Sm(lambda e, buf=buf: e.activation(out=buf, in_=buf, func=AF.Abs), inc="SC")
            P.w("vector", "SC", s2)
            Vm(lambda e, thr=thr, buf=buf: e.tensor_scalar(out=wq_a[:, 0:1024], in0=buf, scalar1=thr, scalar2=None, op0=A.is_ge))
            gm = Vm(lambda e, ob=obuf: e.tensor_tensor(out=ob[:, 0:1024], in0=ob[:, 0:1024], in1=wq_a[:, 0:1024], op=A.mult), inc="V")
        else:
            P.w("vector", "dW%d" % slot, wq_dv[64 + idx])
            P.w("vector", "dQ%d" % oslot, wq_osp[oslot])
            Vm(lambda e, thr=thr, buf=buf: e.tensor_scalar(out=wq_a[:, 0:1024], in0=buf, scalar1=thr, scalar2=None, op0=A.is_ge))
            gm = Vm(lambda e, nthr=nthr, buf=buf, ob=obuf: e.scalar_tensor_tensor(out=ob[:, 0:1024], in0=buf, scalar=nthr,
                                                                                  in1=wq_a[:, 0:1024], op0=A.is_le, op1=A.subtract), inc="V")
        wq_ring[slot] = ("V", gm)
        P.w("sync", "V", gm)
        wq_osp[oslot] = P.dma("sync", "dQ%d" % oslot, dst, obuf[:, 0:1024])
        if "wq" in dtap and which == "W1":
            wq_osp[oslot] = P.dma("sync", "dQ%d" % oslot, dtap["wq"][128 * kc:128 * (kc + 1), 1024 * hf:1024 * (hf + 1)], obuf[:, 0:1024])


    def opsum(h):
        b, sl = (4 + h // 7, h % 7) if h < 14 else (3, h - 14)
        return pba[b][:, 65 * sl:65 * sl + 65]

    v3 = v_sb.rearrange("p (l h c) -> p l h c", l=2, c=65)

    last = {}
    tail_pend = []

    def tail_pump(n):
        for _ in range(n):
            if tail_pend:
                tail_pend.pop(0)()

    def tail_flush():
        while tail_pend:
            tail_pend.pop(0)()

    ln2_pend = []

    def ln2_pump(n):
        for _ in range(n):
            if ln2_pend:
                ln2_pend.pop(0)()

    def ln2_flush():
        while ln2_pend:
            ln2_pend.pop(0)()

    def stile(i, wqjobs=()):
        wqq = list(wqjobs)
        t0 = ST * i

        P.w("gpsimd", "PE", last.get("vproj", 0))
        P.dma("gpsimd", "dG", xT_bf[:, 0:8 * ST].rearrange("p (k t) -> p k t", t=ST),
              dxT[:, t0:t0 + ST].rearrange("(k p) t -> p k t", p=128))
        xg = P.c["dG"]

        P.w("tensor", "dGw")
        P.w("tensor", "dG", xg)
        slot_mark = [0, 0]
        for nm, bpp in (("WqT", bq_pp), ("WkT", bk_pp)):
            for oc in range(8):
                b = (6, 7, 2, 3, 4, 5)[oc % 6]
                P.bank_wait("tensor", b)
                pe_v = None
                for kc in range(8):
                    stat = wT[nm][:, D * kc + 128 * oc:D * kc + 128 * (oc + 1)]
                    mov = xT_bf[:, ST * kc:ST * (kc + 1)]
                    pe_v = Tm(lambda e, o=pba[b], s=stat, m=mov, st_=(kc == 0), sp=(kc == 7):
                              e.matmul(o[:, 0:ST], s, m, start=st_, stop=sp),
                              inc="PE" if kc == 7 else None) or pe_v
                scr = exp_sc[:, ST * (oc % 2):ST * (oc % 2 + 1)]
                rcr = relu_sc[:, ST * (oc % 2):ST * (oc % 2 + 1)]
                bia = bpp[:, oc:oc + 1]
                P.w("scalar", "PE", pe_v)
                P.w("scalar", "V", slot_mark[oc % 2])
                Sm(lambda e, o=scr, p=pba[b], b_=bia: e.activation(out=o, in_=p[:, 0:ST], func=AF.Exp, bias=b_))
                sc_m = Sm(lambda e, o=rcr, p=pba[b], b_=bia: e.activation(out=o, in_=p[:, 0:ST], func=AF.Relu, bias=b_), inc="SC")
                P.w("vector", "SC", sc_m)
                cs = slice(ST * oc, ST * (oc + 1))
                if nm == "WqT":
                    vm = Vm(lambda e, o=qT_sb[:, cs], sc=scr, rc=rcr:
                            e.scalar_tensor_tensor(out=o, in0=sc, scalar=1.0, in1=rc, op0=A.min, op1=A.add), inc="V")
                else:
                    Vm(lambda e, o=kT_e[0:64, cs], sc=scr[0:64, :], rc=rcr[0:64, :]:
                       e.scalar_tensor_tensor(out=o, in0=sc, scalar=1.0, in1=rc, op0=A.min, op1=A.add))
                    vm = Vm(lambda e, o=kT_o[64:128, cs], sc=scr[64:128, :], rc=rcr[64:128, :]:
                            e.scalar_tensor_tensor(out=o, in0=sc, scalar=1.0, in1=rc, op0=A.min, op1=A.add), inc="V")
                slot_mark[oc % 2] = vm
                P.bank_set([b], ("SC", sc_m))
        qk_done = slot_mark[1 - 1], slot_mark[1]
        tail_pump(4)

        kv_marks = []
        for ts in range(2):
            for nm, is_k in (("WkT", True), ("WvT", False)):
                banks = (4, 5) if is_k else (2, 3)
                P.bank_wait("tensor", *banks)
                if not is_k:
                    P.w("tensor", "V", kv_marks[-1])
                pe_v = None
                for kc in range(8):
                    stat = xT_bf[:, ST * kc + 128 * ts:ST * kc + 128 * (ts + 1)]
                    for hf in range(2):
                        mov = wT[nm][:, D * kc + 512 * hf:D * kc + 512 * (hf + 1)]
                        pe_v = Tm(lambda e, o=pba[banks[hf]], s=stat, m=mov, st_=(kc == 0), sp=(kc == 7):
                                  e.matmul(o[:, 0:512], s, m, start=st_, stop=sp),
                                  inc="PE" if (kc == 7 and hf == 1) else None) or pe_v
                P.w("vector", "PE", pe_v)
                if is_k:
                    for hf in range(2):
                        Vm(lambda e, hf=hf, b=banks[hf]: e.tensor_tensor(out=tmp1[:, 512 * hf:512 * (hf + 1)],
                                                                         in0=pba[b][:, 0:512], in1=bk_bc[:, 512 * hf:512 * (hf + 1)], op=A.add))
                    vv = P.op("vector", lambda e: e.nop(), inc="V", dr=True)
                    P.bank_set(banks, ("V", vv))
                    P.w("scalar", "V", vv)
                    Sm(lambda e: e.activation(out=exp_sc, in_=tmp1, func=AF.Exp))
                    sc_m = Sm(lambda e: e.activation(out=relu_sc, in_=tmp1, func=AF.Relu), inc="SC")
                    P.w("vector", "SC", sc_m)
                    vm = Vm(lambda e, o=kt_sb[:, D * ts:D * (ts + 1)]:
                            e.scalar_tensor_tensor(out=o, in0=exp_sc, scalar=1.0, in1=relu_sc, op0=A.min, op1=A.add), inc="V")
                    kv_marks.append(vm)
                else:
                    for hf in range(2):
                        vm = Vm(lambda e, hf=hf, b=banks[hf], o=v3[:, ts, 8 * hf:8 * hf + 8, 0:64]:
                                e.tensor_tensor(out=o, in0=pba[b][:, 0:512], in1=bv_bc[:, 512 * hf:512 * (hf + 1)], op=A.add),
                                inc="V" if hf == 1 else None)
                    kv_marks.append(vm)
                    P.bank_set(banks, ("V", vm))
            last["vproj"] = P.c["PE"]
        kt_done = kv_marks[2]
        v_done = kv_marks[3]
        tail_pump(2)
        nfin = 1 if (wqq and wqq[0] == 64) else 0
        n4 = max(1, (len(wqq) - nfin) // 4) if wqq else 0
        wq_pump(wqq[:nfin + n4])
        del wqq[:nfin + n4]
        if i == n_stiles - 1 == 7 and stage >= 4:
            P.w("sync", "PE", last["vproj"])
            for _q in ("dQ0", "dQ1", "dQ2", "dQ3"):
                P.w("sync", _q)
            for kc in range(6):
                P.dma("sync", "dWq", W1q_sb[:, DH * kc:DH * (kc + 1)], d_w1qsp[128 * kc:128 * (kc + 1), :])

        for lc in range(2):
            c = 2 * i + lc
            pcols = lambda h: slice(ST * (h // 2) + CH * lc, ST * (h // 2) + CH * (lc + 1))
            qfull = lambda h: qT_sb[:, pcols(h)]
            smask = lambda h: (S_e if h % 2 == 0 else S_o)[:, 65 * (h // 2):65 * (h // 2) + 65]
            kmask = lambda h: (kT_e if h % 2 == 0 else kT_o)[:, pcols(h)]
            kth = lambda h: kt_sb[:, D * lc + HD * h:D * lc + HD * (h + 1)]
            vone = lambda h: v3[:, lc, h, :]

            def spsum(h):
                j = h // 2
                return pba[j // 4][64 * (h % 2):64 * (h % 2) + 64, 65 * (j % 4):65 * (j % 4) + 65]

            mask_marks = []
            for g, banks in ((0, (6, 7)), (1, (2, 3))):
                P.bank_wait("tensor", *banks)
                if c == 0 and g == 1:
                    P.w("tensor", "V", v_done)
                pe_sc = None
                for hh in range(8):
                    h = 8 * g + hh
                    pe_sc = Tm(lambda e, o=pba[banks[hh // 4]], s=kmask(h), m=qfull(h), off=128 * (hh % 4):
                               e.matmul(o[:, off:off + 128], s, m, start=True, stop=True),
                               inc="PE" if hh == 7 else None) or pe_sc
                P.w("vector", "PE", pe_sc)
                mm = None
                for bk_ in range(2):
                    mm = Vm(lambda e, g=g, bk_=bk_, b=banks[bk_]: e.tensor_tensor(
                        out=At_sb[:, CH * 8 * g + 512 * bk_:CH * 8 * g + 512 * (bk_ + 1)],
                        in0=pba[b][:, 0:512], in1=mask4, op=A.mult),
                        inc="V" if bk_ == 1 else None) or mm
                mask_marks.append(mm)
                P.bank_set(banks, ("V", mm))

            P.bank_wait("tensor", 3, 4, 5)
            P.w("tensor", "V", mask_marks[1])
            if c > 0:
                P.w("tensor", "V", last["scopy"])
            P.w("tensor", "V", v_done)
            pe_ch = None
            for h in range(16):
                Ath = At_sb[:, CH * h:CH * (h + 1)]
                Tm(lambda e, o=opsum(h), s=Ath, m=vone(h), st_=True, sp=(c == 0):
                   e.matmul(o, s, m, start=st_, stop=sp, skip_group_check=True))
                if c > 0:
                    Tm(lambda e, o=opsum(h), s=qfull(h), m=smask(h):
                       e.matmul(o, s, m, start=False, stop=True, skip_group_check=True))
            P.bank_wait("tensor", 0, 1)
            for h in range(16):
                first = (c == 0) and (h in (0, 1, 8, 9))
                tp = (0, 64) if h % 2 else None
                pe_ch = Tm(lambda e, o=spsum(h), s=kth(h), m=vone(h), st_=first, tp=tp:
                           e.matmul(o, s, m, start=st_, stop=False, tile_position=tp, skip_group_check=True),
                           inc="PE" if h == 15 else None) or pe_ch

            P.w("vector", "PE", pe_ch)
            Vm(lambda e: e.tensor_copy(out=S_e[0:64, 0:260], in_=pba[0][0:64, 0:260]))
            Vm(lambda e: e.tensor_copy(out=S_e[0:64, 260:520], in_=pba[1][0:64, 0:260]))
            Vm(lambda e: e.tensor_copy(out=S_o[64:128, 0:260], in_=pba[0][64:128, 0:260]))
            sm = Vm(lambda e: e.tensor_copy(out=S_o[64:128, 260:520], in_=pba[1][64:128, 0:260]), inc="V")
            last["scopy"] = sm
            P.bank_set([0, 1], ("V", sm))
            Vm(lambda e: e.tensor_scalar(out=st["denf"][:, 0:7], in0=pba[4][:, 0:455].rearrange("p (s c) -> p s c", c=65)[:, :, 64:65], scalar1=EPS_DEN, scalar2=None, op0=A.add))
            Vm(lambda e: e.tensor_scalar(out=st["denf"][:, 7:14], in0=pba[5][:, 0:455].rearrange("p (s c) -> p s c", c=65)[:, :, 64:65], scalar1=EPS_DEN, scalar2=None, op0=A.add))
            Vm(lambda e: e.tensor_scalar(out=st["denf"][:, 14:16], in0=pba[3][:, 0:130].rearrange("p (s c) -> p s c", c=65)[:, :, 64:65], scalar1=EPS_DEN, scalar2=None, op0=A.add))
            Vm(lambda e: e.reciprocal(out=st["denr"], in_=st["denf"]))
            dr3 = lambda a, b: st["denr"][:, a:b].rearrange("p (h o) -> p h o", o=1).broadcast_to([128, b - a, 64])
            onum = lambda bank, n: pba[bank][:, 0:65 * n].rearrange("p (h c) -> p h c", c=65)[:, :, 0:64]
            attn3 = lambda lc, a, b: attn_sb[:, D * lc + HD * a:D * lc + HD * b].rearrange("p (h c) -> p h c", c=64)
            Vm(lambda e, lc=lc: e.tensor_tensor(out=attn3(lc, 0, 7), in0=onum(4, 7), in1=dr3(0, 7), op=A.mult))
            Vm(lambda e, lc=lc: e.tensor_tensor(out=attn3(lc, 7, 14), in0=onum(5, 7), in1=dr3(7, 14), op=A.mult))
            om = Vm(lambda e, lc=lc: e.tensor_tensor(out=attn3(lc, 14, 16), in0=onum(3, 2), in1=dr3(14, 16), op=A.mult), inc="V")
            last["oevac%d" % lc] = om
            P.bank_set([3, 4, 5], ("V", om))
            if lc == 0:
                tail_pump(2)
                wq_pump(wqq[:n4])
                del wqq[:n4]
            else:
                tail_pump(4)

        P.w("vector", "dC")
        tposm = {}
        pair67 = pairB[3]
        for lc in range(2):
            P.w("tensor", "V", last["oevac%d" % lc])
            for j0 in range(0, 8, 2):
                P.bank_wait("tensor", 6, 7)
                for dj in range(2):
                    src = attn_sb[:, D * lc + 128 * (j0 + dj):D * lc + 128 * (j0 + dj + 1)]
                    pe_t = Tm(lambda e, o=pbfa[6 + dj], s=src: e.transpose(o[:, 0:128], s, ident),
                              inc="PE" if dj == 1 else None)
                P.w("vector", "PE", pe_t)
                src2 = pair67[:, 0:1152].rearrange("p (a c) -> p a c", c=128)[:, 0:9:8, :]
                dst2 = attnT_sb[:, ST * j0 + CH * lc:ST * (j0 + 2)].rearrange("p (a c) -> p a c", c=128)[:, 0:3:2, :] if lc == 0 else \
                       attnT_sb[:, ST * j0:ST * (j0 + 2)].rearrange("p (a c) -> p a c", c=128)[:, 1:4:2, :]
                vm = Vm(lambda e, o=dst2, p=src2: e.tensor_copy(out=o, in_=p), inc="V")
                P.bank_set([6, 7], ("V", vm))
            tposm[lc] = vm

        tail_flush()
        P.w("sync", "V", last.get("r1add", 0))
        P.w("sync", "dSp")
        for ts in range(2):
            P.dma("sync", "dX", x_t[:, D * ts:D * (ts + 1)], dx[t0 + 128 * ts:t0 + 128 * (ts + 1), :])
        last["xs"] = P.c["dX"]
        wq_pump(wqq[:n4])
        del wqq[:n4]
        wo_banks = {0: (4, 5), 1: (2, 3)}
        pe_wo = {}
        for ts in range(2):
            bks = wo_banks[ts]
            P.bank_wait("tensor", *bks)
            P.w("tensor", "V", tposm[ts])
            pw = None
            for cc in range(8):
                stat = attnT_sb[:, ST * cc + 128 * ts:ST * cc + 128 * (ts + 1)]
                for hf in range(2):
                    mov = wT["WoT"][:, D * cc + 512 * hf:D * cc + 512 * (hf + 1)]
                    pw = Tm(lambda e, o=pba[bks[hf]], s=stat, m=mov, st_=(cc == 0), sp=(cc == 7):
                            e.matmul(o[:, 0:512], s, m, start=st_, stop=sp),
                            inc="PE" if (cc == 7 and hf == 1) else None) or pw
            pe_wo[ts] = pw

        rm_ts = {}
        for ts in range(2):
            bks = wo_banks[ts]
            P.w("vector", "PE", pe_wo[ts])
            if ts == 0:
                P.w("vector", "dX", last["xs"])
                P.w("vector", "SC", last.get("tail_apm", 0))
            for hf in range(2):
                Vm(lambda e, hf=hf, b=bks[hf]: e.tensor_tensor(out=tmp1[:, 512 * hf:512 * (hf + 1)],
                                                               in0=pba[b][:, 0:512], in1=bo_bc[:, 512 * hf:512 * (hf + 1)], op=A.add))
            rm = Vm(lambda e, ts=ts: e.tensor_tensor(out=r1_sb[:, D * ts:D * (ts + 1)], in0=tmp1, in1=x_t[:, D * ts:D * (ts + 1)], op=A.add), inc="V")
            rm_ts[ts] = rm
            P.bank_set(list(bks), ("V", rm))
        last["r1add"] = rm_ts[1]

        cols = [(2 * i + ts) % 16 for ts in range(2)]
        r1s_ = [r1_sb[:, D * ts:D * (ts + 1)] for ts in range(2)]
        r1l_ = [r1ln_sb[:, D * ts:D * (ts + 1)] for ts in range(2)]
        tbox = {}

        def t_sums(ts):
            def go():
                c = cols[ts]
                P.w("scalar", "V", rm_ts[ts])
                Sm(lambda e, c=c, r=r1s_[ts], o=r1l_[ts]: e.activation(out=o, in_=r, func=AF.Square, accum_out=st["u"][:, c:c + 1]))
                tbox["sc1", ts] = Sm(lambda e, c=c, r=r1s_[ts], o=r1l_[ts]: e.activation(out=o, in_=r, func=AF.Identity, accum_out=st["m"][:, c:c + 1]), inc="SC")
            return go

        def t_tiny(ts):
            def go():
                c = cols[ts]
                P.w("vector", "SC", tbox["sc1", ts])
                Vm(lambda e, c=c: e.tensor_scalar(out=st["m"][:, c:c + 1], in0=st["m"][:, c:c + 1], scalar1=1.0 / D, scalar2=None, op0=A.mult))
                Vm(lambda e, c=c: e.tensor_tensor(out=st["t0"][:, c:c + 1], in0=st["m"][:, c:c + 1], in1=st["m"][:, c:c + 1], op=A.mult))
                tbox["um", ts] = Vm(lambda e, c=c: e.tensor_scalar(out=st["u"][:, c:c + 1], in0=st["u"][:, c:c + 1], scalar1=1.0 / D,
                                                                  scalar2=st["t0"][:, c:c + 1], op0=A.mult, op1=A.subtract), inc="V")
            return go

        def t_lnexp(ts):
            def go():
                c = cols[ts]
                P.w("scalar", "V", tbox["um", ts])
                Sm(lambda e, c=c: e.activation(out=st["t1"][:, c:c + 1], in_=st["u"][:, c:c + 1], func=AF.Ln, bias=st["eps"][:, 0:1]))
                tbox["scm", ts] = Sm(lambda e, c=c: e.activation(out=st["rstd"][:, c:c + 1], in_=st["t1"][:, c:c + 1], func=AF.Exp, scale=-0.5), inc="SC")
            return go

        def t_apply(ts):
            def go():
                c = cols[ts]
                P.w("vector", "SC", tbox["scm", ts])
                nb = Vm(lambda e, c=c: e.tensor_scalar(out=st["nbm"][:, c:c + 1], in0=st["m"][:, c:c + 1],
                                                       scalar1=st["rstd"][:, c:c + 1], scalar2=-1.0, op0=A.mult, op1=A.mult), inc="V")
                P.w("scalar", "V", nb)
                apm = Sm(lambda e, c=c, r=r1s_[ts], o=r1l_[ts]: e.activation(out=o, in_=r, func=AF.Identity,
                                                                             scale=st["rstd"][:, c:c + 1], bias=st["nbm"][:, c:c + 1]), inc="SC")
                tbox["ap", ts] = apm
                last["tail_apm"] = apm
            return go

        def t_amax(ts):
            def go():
                c = cols[ts]
                P.w("vector", "SC", tbox["ap", ts])
                Vm(lambda e, c=c, r=r1l_[ts]: e.reduce_max(out=st["mx2"][:, c:c + 1], in_=r, axis=mybir.AxisListType.X, apply_absolute_value=True))
                Vm(lambda e, c=c: e.tensor_scalar(out=st["mx2"][:, c:c + 1], in0=st["mx2"][:, c:c + 1], scalar1=1e-5, scalar2=None, op0=A.max))
                Vm(lambda e, c=c: e.tensor_scalar(out=st["colraw"][:, c:c + 1], in0=st["mx2"][:, c:c + 1], scalar1=1.0 / 127.0, scalar2=None, op0=A.mult))
                tbox["sa", ts] = Vm(lambda e, c=c: e.reciprocal(out=st["sa"][:, c:c + 1], in_=st["colraw"][:, c:c + 1]), inc="V")
            return go

        def t_quant(ts):
            def go():
                c = cols[ts]
                P.w("scalar", "V", tbox["sa", ts])
                if ts == 1:
                    P.w("scalar", "V", tbox["xm", 0])
                qm = Sm(lambda e, c=c, r=r1l_[ts]: e.activation(out=xq_i32, in_=r, func=AF.Identity, scale=st["sa"][:, c:c + 1]), inc="SC")
                P.w("vector", "SC", qm)
                if ts == 0:
                    P.w("vector", "dC")
                tbox["xm", ts] = Vm(lambda e, o=xq_bf[:, D * ts:D * (ts + 1)]: e.tensor_copy(out=o, in_=xq_i32), inc="V")

                P.w("sync", "V", tbox["sa", ts])
                P.w("sync", "SC", tbox["ap", ts])
                P.dma("sync", "dSp", d_r1sp[t0 + 128 * ts:t0 + 128 * (ts + 1), :], r1l_[ts])
                P.w("sync", "V", tbox["xm", ts])
                P.dma("sync", "dC", d_xq2[1 + t0 + 128 * ts:1 + t0 + 128 * (ts + 1), :], xq_bf[:, D * ts:D * (ts + 1)])
            return go

        tail_pend.extend([
            t_sums(0), t_sums(1), t_tiny(0), t_tiny(1), t_lnexp(0), t_lnexp(1),
            t_apply(0), t_apply(1), t_amax(0), t_quant(0), t_amax(1), t_quant(1)])

        if dtap:
            tail_flush()
            P.w("scalar", "V", P.c["V"])
            P.w("scalar", "SC", P.c["SC"])
        if "r1ln" in dtap:
            for ts in range(2):
                P.dma("scalar", "dC", dtap["r1ln"][t0 + 128 * ts:t0 + 128 * (ts + 1), :], r1ln_sb[:, D * ts:D * (ts + 1)])
        if "attn" in dtap:
            for ts in range(2):
                P.dma("scalar", "dC", dtap["attn"][t0 + 128 * ts:t0 + 128 * (ts + 1), :], attn_sb[:, D * ts:D * (ts + 1)])
        if "xq" in dtap:
            for ts in range(2):
                P.dma("scalar", "dC", dtap["xq"][t0 + 128 * ts:t0 + 128 * (ts + 1), :], xq_bf[:, D * ts:D * (ts + 1)])
        if i == n_stiles - 1:
            while tail_pend or wqq:
                tail_pump(1)
                wq_pump(wqq[:1])
                del wqq[:1]
        else:
            wq_pump(wqq)
            del wqq[:]
        if i == n_stiles - 1 == 7 and stage >= 4:
            P.w("sync", "V", last["r1add"])
            for _q in ("dQ0", "dQ1", "dQ2", "dQ3"):
                P.w("sync", _q)
            for kc in range(6, 8):
                P.dma("sync", "dWq", W1q_sb[:, DH * kc:DH * (kc + 1)], d_w1qsp[128 * kc:128 * (kc + 1), :])

    def ftile(j):
        f0 = FT * j
        WFl = WF
        xqTb = xqTb2[j % 2]
        colsc_row = colsc_row2[j % 2]

        pf = "dPf%d" % (j % 2)
        r1b_sb = r1b2[j % 2]
        if j > 0:
            P.w("sync", "dOut")
            for ts in range(2):
                P.dma("sync", "dX", r1b_sb[:, D * ts:D * (ts + 1)], d_r1sp[f0 + 128 * ts:f0 + 128 * (ts + 1), :])
        dxm = P.c["dX"]

        if j == 0:
            P.w("tensor", pf, last[pf])
            P.bank_wait("tensor", 0)
            cbm_t = Tm(lambda e, colsc_row=colsc_row: e.matmul(pba[0][:, 0:WFl], sc_stat1, colsc_row, start=True, stop=True), inc="PE")
            P.w("vector", "PE", cbm_t)
            cbm = Vm(lambda e: e.tensor_copy(out=colsc_bc, in_=pba[0][:, 0:WFl]), inc="V")
            P.bank_set([0], ("V", cbm))
        else:
            cbm = last["cbmv"]
        if j + 1 < n_ftiles:
            nf0 = FT * (j + 1)
            npf = "dPf%d" % ((j + 1) % 2)
            P.w("sync", "PE", last.get("w1end", 0))
            for kc in range(8):
                P.dma("sync", npf, xqTb2[(j + 1) % 2][:, XW * kc:XW * (kc + 1)], d_xq2[nf0:nf0 + XW, 128 * kc:128 * (kc + 1)], transpose=True)
            P.dma("sync", npf, colsc_row2[(j + 1) % 2], d_colsp[0:1, nf0:nf0 + WFl])
            last[npf] = P.c[npf]

        silu_marks = {}
        hevac = {}
        if j == 0:
            for cc in range(4):
                last["diag0", cc] = Vm(lambda e, cc=cc: e.tensor_scalar(
                    out=diag0_all[:, 128 * cc:128 * (cc + 1)], in0=ident,
                    scalar1=convw_sb[:, 3 * cc:3 * cc + 1], scalar2=None, op0=A.mult), inc="V")
        macc_m = 0
        for cc in range(32):
            bW1 = cc % 2
            bCv = 2 + cc % 2

            P.bank_wait("tensor", bW1)
            if cc == 0:
                P.w("tensor", "V", cbm)
            pe_w1 = None
            for kc in range(8):
                stat = W1q_sb[:, DH * kc + 128 * cc:DH * kc + 128 * (cc + 1)]
                mov = xqTb[:, XW * kc:XW * kc + WFl]
                pe_w1 = Tm(lambda e, o=pba[bW1], s=stat, m=mov, st_=(kc == 0), sp=(kc == 7):
                           e.matmul(o[:, 0:WFl], s, m, start=st_, stop=sp),
                           inc="PE" if kc == 7 else None) or pe_w1
            if cc == 31:
                last["w1end"] = pe_w1
            P.w("vector", "PE", pe_w1)
            hrow = h_sb[:, WFl * cc:WFl * (cc + 1)]
            hm = Vm(lambda e, o=hrow, p=pba[bW1]: e.tensor_tensor(out=o, in0=p[:, 0:WFl], in1=colsc_bc, op=A.mult), inc="V")
            hevac[cc] = hm
            P.bank_set([bW1], ("V", hm))

            if j == 0 and cc + 4 < 32:
                last["diag0", cc + 4] = Vm(lambda e, cc=cc: e.tensor_scalar(
                    out=diag0_all[:, 128 * (cc + 4):128 * (cc + 5)], in0=ident,
                    scalar1=convw_sb[:, 3 * (cc + 4):3 * (cc + 4) + 1], scalar2=None, op0=A.mult), inc="V")
            P.bank_wait("tensor", bCv)
            P.w("tensor", "V", hm)
            if j == 0:
                P.w("tensor", "V", last["diag0", cc])
            pe_cv = Tm(lambda e, o=pba[bCv], s=diag0_all[:, 128 * cc:128 * (cc + 1)],
                       m=h_sb[:, WFl * cc:WFl * cc + FT]:
                       e.matmul(o[:, 0:FT], s, m, start=True, stop=True), inc="PE")
            if cc > 0:
                P.w("vector", "SC", silu_marks[cc - 1])
                hsp = h_sb[:, WFl * (cc - 1) + 1:WFl * (cc - 1) + 1 + FT]
                if cc == 1:
                    Vm(lambda e, hs=hsp: e.tensor_copy(out=maccP[1], in_=hs))
                    macc_m = Vm(lambda e, hs=hsp: e.tensor_copy(out=maccN[1], in_=hs), inc="V")
                else:
                    Vm(lambda e, hs=hsp, cc=cc: e.tensor_tensor(out=maccP[cc % 2], in0=maccP[1 - cc % 2], in1=hs, op=A.max))
                    macc_m = Vm(lambda e, hs=hsp, cc=cc: e.tensor_tensor(out=maccN[cc % 2], in0=maccN[1 - cc % 2], in1=hs, op=A.min), inc="V")
            P.w("vector", "PE", pe_cv)
            cvm = None
            for tap in (1, 2):
                mov = h_sb[:, WFl * cc + tap:WFl * cc + tap + FT]
                cw = convw_sb[:, 3 * cc + tap:3 * cc + tap + 1]
                cvm = Vm(lambda e, o=pba[bCv], m=mov, cw=cw: e.scalar_tensor_tensor(
                    out=o[:, 0:FT], in0=m, scalar=cw, in1=o[:, 0:FT], op0=A.mult, op1=A.add),
                    inc="V" if tap == 2 else None) or cvm
            P.w("scalar", "V", cvm)
            hs = h_sb[:, WFl * cc + 1:WFl * cc + 1 + FT]
            sm = Sm(lambda e, o=hs, p=pba[bCv], b=convb_sb[:, cc:cc + 1]:
                    e.activation(out=o, in_=p[:, 0:FT], func=AF.Silu, bias=b), inc="SC")
            silu_marks[cc] = sm
            P.bank_set([bCv], ("SC", sm))
            if cc in (2, 5, 8, 11, 14, 17, 20, 23):
                ln2_pump(1)
        P.w("vector", "SC", silu_marks[31])
        hsp = h_sb[:, WFl * 31 + 1:WFl * 31 + 1 + FT]
        Vm(lambda e, hs=hsp: e.tensor_tensor(out=maccP[0], in0=maccP[1], in1=hs, op=A.max))
        Vm(lambda e, hs=hsp: e.tensor_tensor(out=maccN[0], in0=maccN[1], in1=hs, op=A.min))
        Vm(lambda e: e.tensor_scalar(out=maccN[1], in0=maccN[0], scalar1=-1.0, scalar2=None, op0=A.mult))
        mh_m = Vm(lambda e: e.tensor_tensor(out=maccP[1], in0=maccP[0], in1=maccN[1], op=A.max), inc="V")

        P.bank_wait("tensor", 2, 3)
        P.w("tensor", "V", mh_m)
        tpm = {}
        for ts in range(2):
            tpm[ts] = Tm(lambda e, o=pbfa[2 + ts], s=maccP[1][:, 128 * ts:128 * (ts + 1)]: e.transpose(o[:, 0:128], s, ident), inc="PE")
        P.w("vector", "PE", tpm[1])
        vm = None
        for ts in range(2):
            col = (2 * j + ts) % 16
            Vm(lambda e, c=col, ts=ts: e.reduce_max(out=st["mhA"][:, c:c + 1], in_=pbfa[2 + ts][:, 0:128], axis=mybir.AxisListType.X))
            Vm(lambda e, c=col: e.tensor_scalar(out=st["mhA"][:, c:c + 1], in0=st["mhA"][:, c:c + 1], scalar1=1e-5, scalar2=None, op0=A.max))
            Vm(lambda e, c=col: e.reciprocal(out=st["t2"][:, c:c + 1], in_=st["mhA"][:, c:c + 1]))
            Vm(lambda e, c=col: e.tensor_scalar(out=st["s2"][:, c:c + 1], in0=st["t2"][:, c:c + 1], scalar1=127.0, scalar2=None, op0=A.mult))
            vm = Vm(lambda e, c=col: e.tensor_tensor(out=st["ev2"][:, c:c + 1], in0=st["mhA"][:, c:c + 1], in1=nb2_pp, op=A.mult), inc="V") or vm
        P.w("tensor", "V", vm)
        s2tp = None
        for ts in range(2):
            col = (2 * j + ts) % 16
            s2tp = Tm(lambda e, c=col, ts=ts: e.transpose(pba[3][0:1, 128 * ts:128 * (ts + 1)], st["s2"][:, c:c + 1], identf),
                      inc="PE" if ts == 1 else None) or s2tp
        P.w("vector", "PE", s2tp)
        s2rm = Vm(lambda e: e.tensor_copy(out=s2_row, in_=pba[3][0:1, 0:FT]), inc="V")
        P.w("tensor", "V", s2rm)
        s2t = Tm(lambda e: e.matmul(pba[2][:, 0:FT], ones_rowf, s2_row, start=True, stop=True), inc="PE")
        P.w("vector", "PE", s2t)
        s2m = Vm(lambda e: e.tensor_copy(out=s2_bc, in_=pba[2][:, 0:FT]), inc="V")
        P.bank_set([2, 3], ("V", s2m))

        if j + 1 < n_ftiles:
            npf = "dPf%d" % ((j + 1) % 2)
            ncol = colsc_row2[(j + 1) % 2]
            P.w("tensor", npf, last[npf])
            P.bank_wait("tensor", 0)
            cbm_t = Tm(lambda e, c=ncol: e.matmul(pba[0][:, 0:WFl], sc_stat1, c, start=True, stop=True), inc="PE")
            P.w("vector", "PE", cbm_t)
            ncb = Vm(lambda e: e.tensor_copy(out=colsc_bc, in_=pba[0][:, 0:WFl]), inc="V")
            P.bank_set([0], ("V", ncb))
            last["cbmv"] = ncb

        ln2_flush()
        if last.get("ln2sc"):
            P.w("vector", "SC", last["ln2sc"])
        if j == 0:
            P.w("tensor", "dWq2", P.c["dWq2"])
        P.bank_wait("tensor", 4, 5, 6, 7)
        qv_marks = {}
        pe_w2 = None
        for cc in range(32):
            hs = h_sb[:, WFl * cc + 1:WFl * cc + 1 + FT]
            P.w("vector", "SC", qv_marks.get(("sc", cc - 2), 0))
            qs = q_scr[:, FT * (cc % 2):FT * (cc % 2 + 1)]
            hq = hq_i32[:, FT * (cc % 2):FT * (cc % 2 + 1)]
            vt = Vm(lambda e, hs=hs, qs=qs: e.tensor_tensor(out=qs, in0=hs, in1=s2_bc, op=A.mult), inc="V")
            P.w("scalar", "V", vt)
            scm = Sm(lambda e, qs=qs, hq=hq: e.activation(out=hq, in_=qs, func=AF.Identity), inc="SC")
            qv_marks[("sc", cc)] = scm
            P.w("vector", "SC", scm)
            cm = Vm(lambda e, hs=hs, hq=hq: e.tensor_copy(out=hs, in_=hq), inc="V")
            qv_marks[cc] = cm
            P.w("tensor", "V", cm)
            for ts in range(2):
                stat = h_sb[:, WFl * cc + 1 + 128 * ts:WFl * cc + 1 + 128 * (ts + 1)]
                for hf in range(2):
                    bb = 4 + 2 * ts + hf
                    mov = W2q_sb[:, D * cc + 512 * hf:D * cc + 512 * (hf + 1)]
                    pe_w2 = Tm(lambda e, o=pba[bb], s=stat, m=mov, st_=(cc == 0), sp=(cc == 31):
                               e.matmul(o[:, 0:512], s, m, start=st_, stop=sp),
                               inc="PE" if (cc == 31 and ts == 1 and hf == 1) else None) or pe_w2
        last["w2"] = pe_w2

        box = {"lnm": None, "apm": None}

        def _evac(ts, col):
            def go():
                if ts == 0:
                    P.w("vector", "PE", pe_w2)
                    P.w("vector", "dX")
                else:
                    P.w("vector", "SC", box["apm"])
                lnm = None
                for hf in range(2):
                    bb = 4 + 2 * ts + hf
                    lnm = Vm(lambda e, c=col, bb=bb, hf=hf, ts=ts: e.scalar_tensor_tensor(
                        out=r2_sb[:, 512 * hf:512 * (hf + 1)], in0=pba[bb][:, 0:512],
                        scalar=st["ev2"][:, c:c + 1], in1=r1b_sb[:, D * ts + 512 * hf:D * ts + 512 * (hf + 1)],
                        op0=A.mult, op1=A.add), inc="V" if hf == 1 else None) or lnm
                box["lnm"] = lnm
            return go

        def _sums(ts, col):
            def go():
                P.w("scalar", "V", box["lnm"])
                Sm(lambda e, c=col: e.activation(out=ln2_junk, in_=r2_sb, func=AF.Square, accum_out=st["u2"][:, c:c + 1]))
                sc1 = Sm(lambda e, c=col: e.activation(out=ln2_junk, in_=r2_sb, func=AF.Identity, accum_out=st["m2"][:, c:c + 1]), inc="SC")
                box["sc1"] = sc1
            return go

        def _stat(ts, col):
            def go():
                P.w("vector", "SC", box["sc1"])
                Vm(lambda e, c=col: e.tensor_scalar(out=st["m2"][:, c:c + 1], in0=st["m2"][:, c:c + 1], scalar1=1.0 / D, scalar2=None, op0=A.mult))
                Vm(lambda e, c=col: e.tensor_tensor(out=st["t3"][:, c:c + 1], in0=st["m2"][:, c:c + 1], in1=st["m2"][:, c:c + 1], op=A.mult))
                um = Vm(lambda e, c=col: e.tensor_scalar(out=st["u2"][:, c:c + 1], in0=st["u2"][:, c:c + 1], scalar1=1.0 / D,
                                                         scalar2=st["t3"][:, c:c + 1], op0=A.mult, op1=A.subtract), inc="V")
                P.w("scalar", "V", um)
                Sm(lambda e, c=col: e.activation(out=st["t3"][:, c:c + 1], in_=st["u2"][:, c:c + 1], func=AF.Ln, bias=st["eps"][:, 0:1]))
                scm = Sm(lambda e, c=col: e.activation(out=st["rstd2"][:, c:c + 1], in_=st["t3"][:, c:c + 1], func=AF.Exp, scale=-0.5), inc="SC")
                box["scm"] = scm
            return go

        def _apply(ts, col):
            def go():
                P.w("vector", "SC", box["scm"])
                nb = Vm(lambda e, c=col: e.tensor_scalar(out=st["nbm"][:, c:c + 1], in0=st["m2"][:, c:c + 1],
                                                         scalar1=st["rstd2"][:, c:c + 1], scalar2=-1.0, op0=A.mult, op1=A.mult), inc="V")
                P.w("scalar", "V", nb)
                apm = Sm(lambda e, c=col, ts=ts: e.activation(out=r1b_sb[:, D * ts:D * (ts + 1)], in_=r2_sb, func=AF.Identity,
                                                              scale=st["rstd2"][:, c:c + 1], bias=st["nbm"][:, c:c + 1]), inc="SC")
                box["apm"] = apm
                P.w("sync", "SC", apm)
                P.dma("sync", "dOut", dout[f0 + 128 * ts:f0 + 128 * (ts + 1), :], r1b_sb[:, D * ts:D * (ts + 1)])
                if ts == 1:
                    last["ln2"] = box["lnm"]
                    last["ln2sc"] = apm
                    P.bank_set([4, 5, 6, 7], ("V", box["lnm"]))
            return go

        for ts in range(2):
            col = (2 * j + ts) % 16
            ln2_pend.append(_evac(ts, col))
            ln2_pend.append(_sums(ts, col))
            ln2_pend.append(_stat(ts, col))
            ln2_pend.append(_apply(ts, col))

    prologue()
    if stage >= 4 and n_stiles == 8:
        quotas = [12, 24, 28, 17, 15, 15, 14, 4]
        steps = list(range(129))
        sched = []
        p = 0
        for q in quotas:
            sched.append(steps[p:p + q])
            p += q
        while wq_ld[0] < WQ_AHEAD:
            wq_issue_one()
    else:
        sched = [[] for _ in range(n_stiles)]
    for i in range(n_stiles):
        stile(i, wqjobs=sched[i])

    if stage >= 4:
        tail_flush()
        P.bank_wait("tensor", 0, 1)
        P.w("tensor", "V", P.c["V"])
        cr_t = Tm(lambda e: e.transpose(pairA[0][0:16, 0:128], st["colraw"], identf), inc="PE")
        P.w("vector", "PE", cr_t)
        crv = Vm(lambda e: e.tensor_copy(out=colrow_sb, in_=pairA[0][0:16, 0:128]), inc="V")
        P.bank_set([0, 1], ("V", crv))
        P.w("sync", "V", crv)
        for i8 in range(8):
            for ts8 in range(2):
                seg = (2 * i8 + ts8) % 16
                P.dma("sync", "dSp", d_colsp[0:1, 1 + 256 * i8 + 128 * ts8:1 + 256 * i8 + 128 * (ts8 + 1)],
                      colrow_sb[seg:seg + 1, 0:128])
        P.w("sync", "V", P.c["V"])
        P.w("sync", "dC")
        P.w("sync", "dSp")
        for kc in range(8):
            P.dma("sync", "dPf0", xqTb2[0][:, XW * kc:XW * (kc + 1)], d_xq2[0:XW, 128 * kc:128 * (kc + 1)], transpose=True)
        P.dma("sync", "dPf0", colsc_row2[0], d_colsp[0:1, 0:WF])
        last["dPf0"] = P.c["dPf0"]
        for ts in range(2):
            P.dma("sync", "dX", r1b2[0][:, D * ts:D * (ts + 1)], d_r1sp[128 * ts:128 * (ts + 1), :])
        P.barrier("AB")
        P.w("sync", "dSp")
        P.w("sync", "dC")
        for _q in ("dQ0", "dQ1", "dQ2", "dQ3"):
            P.w("sync", _q)
        if n_stiles != 8:
            for kc in range(8):
                P.dma("sync", "dWq", W1q_sb[:, DH * kc:DH * (kc + 1)], d_w1qsp[128 * kc:128 * (kc + 1), :])
        for cc in range(32):
            P.dma("sync", "dWq2", W2q_sb[:, D * cc:D * (cc + 1)], d_w2qsp[128 * cc:128 * (cc + 1), :])
        P.w("tensor", "dWq", P.c["dWq"])
        for j in range(n_ftiles):
            ftile(j)
        ln2_flush()

    for _sm in ("dS", "dC", "dG", "dGw", "dX", "dW0", "dW1", "dW2", "dW3", "dW4", "dW5", "dQ0", "dQ1", "dQ2", "dQ3", "dH", "dWq", "dWq2", "dSp", "dOut", "dPf0", "dPf1"):
        P.w("sync", _sm)

    import contextlib
    with contextlib.ExitStack() as es:
        block = es.enter_context(nc.Block())

        @block.sync
        def _(e):
            for fn in P.st["sync"]:
                fn(e)

        @block.gpsimd
        def _(e):
            for fn in P.st["gpsimd"]:
                fn(e)

        @block.scalar
        def _(e):
            for fn in P.st["scalar"]:
                fn(e)

        @block.vector
        def _(e):
            for fn in P.st["vector"]:
                fn(e)

        @block.tensor
        def _(e):
            for fn in P.st["tensor"]:
                fn(e)

    return nc


def host_inputs(inp, core):
    f = lambda a: np.ascontiguousarray(np.asarray(a, dtype=np.float32))
    x = f(inp["L0_attn_x"])[core]
    return {
        "x": x,
        "xT": np.ascontiguousarray(x.T),
        "WqT": np.ascontiguousarray(f(inp["L0_attn_Wq"]).T),
        "WkT": np.ascontiguousarray(f(inp["L0_attn_Wk"]).T),
        "WvT": np.ascontiguousarray(f(inp["L0_attn_Wv"]).T),
        "WoT": np.ascontiguousarray(f(inp["L0_attn_Wo"]).T),
        "bq": f(inp["L0_attn_bq"]), "bk": f(inp["L0_attn_bk"]),
        "bv": f(inp["L0_attn_bv"]), "bo": f(inp["L0_attn_bo"]),
        "W1T": np.ascontiguousarray(f(inp["L0_ffn_w1"]).T),
        "W2T": np.ascontiguousarray(f(inp["L0_ffn_w2"]).T),
        "convw": np.ascontiguousarray(f(inp["L0_ffn_conv_w"])[:, 0, :].reshape(32, 128, 3).transpose(1, 0, 2).reshape(128, 96)),
        "convb": np.ascontiguousarray(f(inp["L0_ffn_conv_b"]).reshape(32, 128).T),
    }
def kernel(**inputs):
    nc = build_program(stage=4, taps=(), n_stiles=NST, n_ftiles=NFT)
    in_maps = [host_inputs(inputs, b) for b in range(8)]
    res = run_bass_kernel_spmd(nc, in_maps, core_ids=list(range(8)))
    out = np.stack([np.asarray(res.results[b]["out"], dtype=np.float32) for b in range(8)], axis=0)
    return out
